# version 03
# baseline (speedup 1.0000x reference)
import contextlib
import numpy as np
import concourse.bass as bass
import concourse.mybir as mybir
from concourse.bass_utils import run_bass_kernel_spmd

F32 = mybir.dt.float32
BF16 = mybir.dt.bfloat16
AF = mybir.ActivationFunctionType
ALU = mybir.AluOpType
AX = mybir.AxisListType

H, KD, VD, D, HID, CH, BLK, KS = 8, 128, 256, 1024, 4096, 128, 256, 4
NCORES = 8

QC = {("tensor", False): "t", ("vector", False): "v", ("scalar", False): "s",
      ("gpsimd", False): "g", ("gpsimd", True): "gd", ("sync", True): "sd"}
QUEUES = ["tensor", "vector", "scalar", "gpsimd", "sync"]
DMACTRS = ("sd", "gd")


class Em:

    def __init__(self):
        self.paranoid = False
        self.q = {qn: [] for qn in QUEUES}
        self.cnt = {c: 0 for c in QC.values()}
        self.lw = {}
        self.rd = {}

    def emit(self, queue, fn, reads=(), writes=(), dma=False, selfsync=False):
        ctr = QC[(queue, dma)]
        step = 16 if ctr in DMACTRS else 1
        pos = self.cnt[ctr] + step
        self.cnt[ctr] = pos
        waits = {}

        if self.paranoid:
            selfsync = True

        def addw(c, p):
            if (c != ctr or selfsync) and waits.get(c, 0) < p:
                waits[c] = p

        for r in reads:
            w = self.lw.get(r)
            if w:
                addw(*w)
        for r in writes:
            w = self.lw.get(r)
            if w:
                addw(*w)
            for (c, p) in self.rd.get(r, ()):
                addw(c, p)
        self.q[queue].append(dict(fn=fn, waits=waits, ctr=ctr, pos=pos))
        for r in reads:
            self.rd.setdefault(r, []).append((ctr, pos))
        for r in writes:
            self.lw[r] = (ctr, pos)
            self.rd[r] = []

    def barrier(self):
        snap = dict(self.cnt)
        for qn in QUEUES:
            for ctr, pos in snap.items():
                if pos > 0:
                    self.q[qn].append(dict(fn=None, waits={ctr: pos}, ctr=None, pos=None))
        self.lw = {}
        self.rd = {}

    def finalize(self, nc):
        waited = {c: set() for c in QC.values()}
        for qn in QUEUES:
            for rec in self.q[qn]:
                for c, p in rec["waits"].items():
                    waited[c].add(p)
        rank = {}
        for c in ("t", "v", "s", "g"):
            rank[c] = {p: i + 1 for i, p in enumerate(sorted(waited[c]))}
        sems = {}
        ctxs = []
        for c in QC.values():
            cm = nc.semaphore(f"sem_{c}")
            ctxs.append(cm)
            sems[c] = cm.__enter__()

        def body_for(qn):
            def body(eng):
                lastw = {}
                for rec in self.q[qn]:
                    for c, p in sorted(rec["waits"].items()):
                        val = p if c in DMACTRS else rank[c][p]
                        if lastw.get(c, -1) < val:
                            eng.wait_ge(sems[c], val)
                            lastw[c] = val
                    if rec["fn"] is None:
                        continue
                    inst = rec["fn"](eng)
                    c = rec["ctr"]
                    if c in DMACTRS:
                        inst.then_inc(sems[c], 16)
                    elif rec["pos"] in waited[c]:
                        inst.then_inc(sems[c], 1)
            return body

        with nc.Block() as block:
            block.tensor(body_for("tensor"))
            block.vector(body_for("vector"))
            block.scalar(body_for("scalar"))
            block.gpsimd(body_for("gpsimd"))
            block.sync(body_for("sync"))
        for cm in reversed(ctxs):
            cm.__exit__(None, None, None)


def build_nc(cfg):
    nseq = cfg["nseq"]
    nblk = cfg["nblk"]
    do_pass2 = cfg["do_pass2"]
    Tloc = nblk * BLK
    nc = bass.Bass("TRN2", target_bir_lowering=False)
    em = Em()
    st = contextlib.ExitStack()

    def din(name, shape, dt=F32):
        return nc.dram_tensor(name, shape, dt, kind="ExternalInput")

    X = din("x", [nseq, Tloc, D])
    WQ = din("wq", [D, D])
    WK = din("wk", [D, D])
    WV = din("wv", [D, 2048])
    WG = din("wg", [D, 2048])
    WA = din("wa", [D, 32])
    WB = din("wb", [D, 32])
    WO = din("wo", [2048, D])
    W1T = din("w1t", [D, HID])
    W2T = din("w2t", [HID, D])
    CIDENT = din("c_ident", [128, 128])
    CMASKUS = din("c_maskus", [128, 128])
    CMASKLS = din("c_maskls", [128, 128])
    CMASKUI = din("c_maskui", [128, 128])
    CMBD = din("c_mbd", [128, 128])
    CMSBU = din("c_msbu", [128, 128])
    CONES32 = din("c_ones32", [32, 128])
    CEPICK = din("c_epick", [32, 8])
    CSELQ = din("c_selq", [128, 64])
    CONES1R = din("c_ones1r", [1, 128])
    CONESCOL = din("c_onescol", [128, 1])
    CCPOOL = din("c_cpool", [128, 8])
    CCONVW = din("c_convw", [128, 128])
    CONORM = din("c_onorm", [128, 2])
    CABC = din("c_abc", [32, 2])
    CG1 = din("c_g1bc", [128, D])
    CB1 = din("c_b1bc", [128, D])
    CB2 = din("c_b2bc", [128, D])
    CGB2 = din("c_gb2", [128, 64])
    CB1COL = din("c_b1col", [128, 32])
    CG2COL = din("c_g2col", [128, 32])

    OUT = nc.dram_tensor("out", [nseq, Tloc, D], F32, kind="ExternalOutput")
    DBG = {}
    if cfg.get("dump"):
        for nm, shp, dt_ in [("nnp", [128, 128], BF16), ("nn", [128, 128], BF16),
                             ("mt", [128, 128], BF16), ("x1", [128, 128], BF16),
                             ("y0", [128, 128], BF16), ("y1", [128, VD], BF16),
                             ("ub", [128, VD], BF16), ("gbc", [128, 128], BF16),
                             ("thr", [128, 8], F32), ("xn2", [128, D], BF16),
                             ("wcolsd", [128, 8], F32), ("sums", [128, 2], F32),
                             ("hall", [128, 32 * BLK], BF16), ("fwpre", [128, D], F32),
                             ("cgbc", [128, D], F32), ("b2cb", [128, D], F32),
                             ("G1s", [128, HID], BF16), ("G2s", [128, 4 * 257], BF16),
                             ("qs", [128, 8 * BLK], BF16), ("ks", [128, 8 * BLK], BF16),
                             ("vs", [128, 16 * BLK], BF16), ("sgate", [128, 16 * 128], BF16),
                             ("grow", [32, BLK], F32), ("lnbrow", [32, BLK], F32),
                             ("lia", [32, BLK], F32), ("lik", [32, BLK], F32),
                             ("rowstack", [32, 512], F32), ("oall", [128, 16 * 128], BF16),
                             ("aw", [128, D], F32), ("stb", [128, 16 * VD], BF16),
                             ("xnt", [128, 8 * BLK], BF16)]:
            DBG[nm] = nc.dram_tensor("dbg_" + nm, shp, dt_, kind="ExternalOutput")
    XP = nc.dram_tensor("xprime", [nseq, Tloc, D], F32,
                        kind="Internal" if do_pass2 else "ExternalOutput")
    TERN1 = nc.dram_tensor("tern1", [D, HID], BF16, kind="Internal")
    TERN2 = nc.dram_tensor("tern2", [HID, D], BF16, kind="Internal")


    def sb(name, shape, dt):
        return st.enter_context(nc.sbuf_tensor(name, shape, dt))

    ident_b = sb("ident_b", [128, 128], BF16)
    ident_f = sb("ident_f", [32, 32], F32)
    maskus = sb("maskus", [128, 128], BF16)
    maskls = sb("maskls", [128, 128], BF16)
    maskui = sb("maskui", [128, 128], BF16)
    mbd = sb("mbd", [128, 128], BF16)
    mbdT = sb("mbdT", [128, 128], BF16)
    msbu = sb("msbu", [128, 128], BF16)
    ones32 = sb("ones32", [32, 128], F32)
    epick = sb("epick", [32, 8], F32)
    selq = sb("selq", [128, 64], BF16)
    ones1r = sb("ones1r", [1, 128], BF16)
    ones1rf = sb("ones1rf", [1, 128], F32)
    onescol = sb("onescol", [128, 1], BF16)
    cpool = sb("cpool", [128, 8], F32)
    convw = sb("convw", [128, 128], F32)
    onorm = sb("onorm", [128, 2], F32)
    abc = sb("abc", [32, 2], F32)
    negexpA = sb("negexpA", [32, 1], F32)
    thr = sb("thr", [128, 8], F32)
    accbuf = sb("accbuf", [128, 4], F32)
    tcell = sb("tcell", [1, 2], F32)

    stb = sb("stb", [128, 16, VD], BF16)

    pbank = []
    for i, (nm, shp, dt) in enumerate([
            ("psA", [128, 512], F32), ("psB", [128, 512], F32),
            ("psC", [128, 512], F32), ("psD", [128, 512], F32),
            ("psT", [128, 1024], BF16), ("psU", [128, 512], F32),
            ("psO", [128, 512], F32), ("psW", [128, 512], F32)]):
        pbank.append(st.enter_context(nc.psum_tensor(nm, shp, dt)))
    psA, psB, psC, psD, psT, psU, psO, psW = pbank
    RB = ["pb0", "pb1", "pb2", "pb3", "pb4", "pb5", "pb6", "pb7"]

    p1 = contextlib.ExitStack()

    def sb1(name, shape, dt):
        return p1.enter_context(nc.sbuf_tensor(name, shape, dt))

    wq_s = sb1("wq_s", [128, 8, D], BF16)
    wk_s = sb1("wk_s", [128, 8, D], BF16)
    wv_s = sb1("wv_s", [128, 8, 2048], BF16)
    wg_s = sb1("wg_s", [128, 8, 2048], BF16)
    wa_s = sb1("wa_s", [128, 8, 32], BF16)
    wb_s = sb1("wb_s", [128, 8, 32], BF16)
    wo_s = sb1("wo_s", [128, 16, D], BF16)
    xin = sb1("xio", [128, D], F32)
    xres = xin
    xnorm = sb1("xnorm", [128, 512], BF16)
    xnt = sb1("xnt", [128, 8, BLK], BF16)
    qs = sb1("qs", [128, 8, BLK], BF16)
    ks = sb1("ks", [128, 8, BLK], BF16)
    vs = sb1("vs", [128, 16, BLK], BF16)
    sgate = sb1("sgate", [128, 16, 128], BF16)
    zbuf = sb1("zbuf", [128, 2, BLK + 8], BF16)
    cva = sb1("cva", [128, 2, BLK], BF16)
    cvb = sb1("cvb", [128, 2, BLK], BF16)
    halo = sb1("halo", [128, 32, 4], BF16)
    zarow = sb1("zarow", [32, BLK], F32)
    spw0 = sb1("spw0", [32, BLK], F32)
    grow = sb1("grow", [32, BLK], F32)
    lnbrow = sb1("lnbrow", [32, BLK], F32)
    lia = sb1("lia", [32, BLK], F32)
    lik = sb1("lik", [32, BLK], F32)
    rowstack = sb1("rowstack", [32, 1, 512], F32)
    colstack = sb1("colstack", [128, 1, 96], F32)
    nrb = sb1("nrb", [128, 1, 8], F32)
    gbh = sb1("gbh", [128, 1, 8], F32)
    bcols = sb1("bcols", [128, 1, 8], F32)
    NG = 2
    zpick = sb1("zpick", [32, NG, 128], F32)
    zpickc = sb1("zpickc", [32, NG, 1], F32)
    ktb = sb1("ktb", [128, NG, 128], BF16)
    y0b = sb1("y0b", [128, NG, 128], BF16)
    y1b = sb1("y1b", [128, NG, VD], BF16)
    dkb = sb1("dkb", [128, NG, 128], BF16)
    nkb = sb1("nkb", [128, NG, 128], BF16)
    nnb = sb1("nnb", [128, NG, 128], BF16)
    nnpb = sb1("nnpb", [128, NG, 128], BF16)
    mtb = sb1("mtb", [128, NG, 128], BF16)
    ebufb = sb1("ebufb", [128, NG, 128], F32)
    rqbuf = sb1("rqbuf", [128, NG, 4 * 128], BF16)
    pcb = sb1("pcb", [128, NG, 2, 128], BF16)
    x1b = sb1("x1b", [128, NG, 128], BF16)
    xbdb = sb1("xbdb", [128, NG, 128], BF16)
    mlitb = sb1("mlitb", [128, NG, 128], BF16)
    astb = sb1("astb", [128, NG, 128], BF16)
    wutb = sb1("wutb", [128, NG, 128], BF16)
    ubb = sb1("ubb", [128, NG, VD], BF16)
    gbcb = sb1("gbcb", [128, NG, 128], BF16)
    rqb = sb1("rqb", [128, NG, 128], BF16)
    rrow = sb1("rrow", [1, NG, 128], F32)
    rrowb = sb1("rrowb", [1, NG, 128], BF16)
    l127c = sb1("l127c", [128, NG, 1], F32)
    dgc = sb1("dgc", [128, NG, 1], F32)
    gamc = sb1("gamc", [128, NG, 1], F32)
    o_all = sb1("o_all", [128, 1, 16, 128], BF16)
    aw = sb1("aw", [128, D], F32)
    lnst = sb1("lnst", [128, 12], F32)
    mvst = sb1("mvst", [128, 2], F32)
    wcols = sb1("wcols", [128, 8], F32)
    wstage = sb1("wstage", [128, 256], F32)
    wabs = sb1("wabs", [128, 256], BF16)
    wtern = sb1("wtern", [128, 256], BF16)

    def T(fn, r, w):
        em.emit("tensor", fn, r, w)

    def V(fn, r, w, selfsync=False):
        em.emit("vector", fn, r, w, selfsync=selfsync)

    def S(fn, r, w, selfsync=True):
        em.emit("scalar", fn, r, w, selfsync=selfsync)

    def G(fn, r, w):
        em.emit("gpsimd", fn, r, w)

    def SD(fn, r, w):
        em.emit("sync", fn, r, w, dma=True)

    def GD(fn, r, w):
        em.emit("gpsimd", fn, r, w, dma=True)

    def pe_fence(bank_ap, bank_res, off):
        T(lambda e, p=bank_ap, o2=off: e.matmul(p[0:1, o2:o2 + 1], onescol[:, 0:1], onescol[:, 0:1],
                                                start=True, stop=True), ["onescol"], [bank_res])

    SD(lambda e: e.dma_start(out=ident_f[:, :], in_=CIDENT[0:32, 0:32]), [], ["ident_f"])
    GD(lambda e: e.dma_start(out=ident_b[:, :], in_=CIDENT.ap()), [], ["ident_b"])
    GD(lambda e: e.dma_start(out=maskus[:, :], in_=CMASKUS.ap()), [], ["maskus"])
    GD(lambda e: e.dma_start(out=maskls[:, :], in_=CMASKLS.ap()), [], ["maskls"])
    GD(lambda e: e.dma_start(out=maskui[:, :], in_=CMASKUI.ap()), [], ["maskui"])
    GD(lambda e: e.dma_start(out=mbd[:, :], in_=CMBD.ap()), [], ["mbd"])
    GD(lambda e: e.dma_start(out=mbdT[:, :], in_=CMBD.ap()), [], ["mbdT"])
    GD(lambda e: e.dma_start(out=msbu[:, :], in_=CMSBU.ap()), [], ["msbu"])
    SD(lambda e: e.dma_start(out=ones32[:, :], in_=CONES32.ap()), [], ["ones32"])
    SD(lambda e: e.dma_start(out=epick[:, :], in_=CEPICK.ap()), [], ["epick"])
    GD(lambda e: e.dma_start(out=selq[:, :], in_=CSELQ.ap()), [], ["selq"])
    GD(lambda e: e.dma_start(out=ones1r[:, :], in_=CONES1R.ap()), [], ["ones1r"])
    SD(lambda e: e.dma_start(out=ones1rf[:, :], in_=CONES1R.ap()), [], ["ones1rf"])
    GD(lambda e: e.dma_start(out=onescol[:, :], in_=CONESCOL.ap()), [], ["onescol"])
    SD(lambda e: e.dma_start(out=cpool[:, :], in_=CCPOOL.ap()), [], ["cpool"])
    SD(lambda e: e.dma_start(out=convw[:, :], in_=CCONVW.ap()), [], ["convw"])
    SD(lambda e: e.dma_start(out=onorm[:, :], in_=CONORM.ap()), [], ["onorm"])
    SD(lambda e: e.dma_start(out=abc[:, :], in_=CABC.ap()), [], ["abc"])
    for dt_ in range(8):
        GD(lambda e, d=dt_: e.dma_start(out=wa_s[:, d, :], in_=WA[d * 128:(d + 1) * 128, :]), [], ["wa_s"])
        GD(lambda e, d=dt_: e.dma_start(out=wb_s[:, d, :], in_=WB[d * 128:(d + 1) * 128, :]), [], ["wb_s"])
    for dt_ in range(8):
        GD(lambda e, d=dt_: e.dma_start(out=wq_s[:, d, :], in_=WQ[d * 128:(d + 1) * 128, :]), [], ["wq_s"])
        GD(lambda e, d=dt_: e.dma_start(out=wk_s[:, d, :], in_=WK[d * 128:(d + 1) * 128, :]), [], ["wk_s"])
        GD(lambda e, d=dt_: e.dma_start(out=wv_s[:, d, :], in_=WV[d * 128:(d + 1) * 128, :]), [], ["wv_s"])
        GD(lambda e, d=dt_: e.dma_start(out=wg_s[:, d, :], in_=WG[d * 128:(d + 1) * 128, :]), [], ["wg_s"])
    for vt in range(16):
        GD(lambda e, v=vt: e.dma_start(out=wo_s[:, v, :], in_=WO[v * 128:(v + 1) * 128, :]), [], ["wo_s"])

    S(lambda e: e.activation(negexpA[:, 0:1], abc[:, 0:1], AF.Exp), ["abc"], ["negexpA"])
    V(lambda e: e.tensor_scalar(out=negexpA[:, 0:1], in0=negexpA[:, 0:1], scalar1=-1.0,
                                scalar2=None, op0=ALU.mult), ["negexpA"], ["negexpA"])
    V(lambda e: e.memset(accbuf[:, :], 0.0), [], ["accbuf"])
    V(lambda e: e.memset(lia[:, :], 0.0), [], ["lia"])
    V(lambda e: e.memset(lik[:, :], 0.0), [], ["lik"])

    quant_steps = []

    def q_mean_slab(W_, dti, cs, half):
        def go():
            SD(lambda e: e.dma_start(out=wstage[:, :], in_=W_[dti * 128:(dti + 1) * 128, cs:cs + 256]),
               ["wstage"], ["wstage"])
            S(lambda e: e.activation(wabs[:, :], wstage[:, :], AF.Abs,
                                     accum_out=wcols[:, 3:4]), ["wstage"], ["wabs", "wcols"])
            G(lambda e, hf=half: e.tensor_tensor(accbuf[:, hf:hf + 1], accbuf[:, hf:hf + 1],
                                                 wcols[:, 3:4], ALU.add), ["accbuf", "wcols"], ["accbuf"])
        return go

    def q_thresh(half, nelem):
        def go():
            T(lambda e, hf=half: e.matmul(psW[0:1, 0:1], cpool[:, 0:1], accbuf[:, hf:hf + 1],
                                          start=True, stop=True), ["accbuf", "cpool"], [RB[7]])
            pe_fence(psW, RB[7], 500)
            V(lambda e: e.tensor_scalar(out=tcell[0:1, 0:1], in0=psW[0:1, 0:1],
                                        scalar1=1.0 / nelem, scalar2=1e-5, op0=ALU.mult, op1=ALU.max),
              [RB[7]], ["tcell"])
            T(lambda e: e.matmul(psW[:, 0:1], ones1rf[:, :], tcell[0:1, 0:1],
                                 start=True, stop=True), ["tcell", "ones1rf"], [RB[7]])
            base = half * 3
            pe_fence(psW, RB[7], 501)
            V(lambda e, b3=base: e.tensor_scalar(out=thr[:, b3:b3 + 1], in0=psW[:, 0:1], scalar1=0.5,
                                                 scalar2=None, op0=ALU.mult), [RB[7]], ["thr"])
            V(lambda e, b3=base: e.tensor_scalar(out=thr[:, b3 + 1:b3 + 2], in0=psW[:, 0:1], scalar1=-0.5,
                                                 scalar2=None, op0=ALU.mult), [RB[7]], ["thr"])
            V(lambda e, b3=base: e.tensor_copy(thr[:, b3 + 2:b3 + 3], psW[:, 0:1]), [RB[7]], ["thr"])
        return go

    def q_tern_slab(W_, TOUT, dti, cs, half, scale_mean):
        def go():
            SD(lambda e: e.dma_start(out=wstage[:, :], in_=W_[dti * 128:(dti + 1) * 128, cs:cs + 256]),
               ["wstage"], ["wstage"])
            base = half * 3
            G(lambda e, b3=base: e.tensor_scalar(out=wabs[:, :], in0=wstage[:, :], scalar1=thr[:, b3:b3 + 1],
                                                 scalar2=None, op0=ALU.is_ge), ["wstage", "thr"], ["wabs"])
            G(lambda e, b3=base: e.tensor_scalar(out=wtern[:, :], in0=wstage[:, :], scalar1=thr[:, b3 + 1:b3 + 2],
                                                 scalar2=None, op0=ALU.is_le), ["wstage", "thr"], ["wtern"])
            if scale_mean:
                G(lambda e: e.tensor_tensor(wabs[:, :], wabs[:, :], wtern[:, :], ALU.subtract),
                  ["wabs", "wtern"], ["wabs"])
                G(lambda e, b3=base: e.tensor_scalar(out=wtern[:, :], in0=wabs[:, :], scalar1=thr[:, b3 + 2:b3 + 3],
                                                     scalar2=None, op0=ALU.mult), ["wabs", "thr"], ["wtern"])
            else:
                G(lambda e: e.tensor_tensor(wtern[:, :], wabs[:, :], wtern[:, :], ALU.subtract),
                  ["wabs", "wtern"], ["wtern"])
            SD(lambda e: e.dma_start(out=TOUT[dti * 128:(dti + 1) * 128, cs:cs + 256], in_=wtern[:, :]),
               ["wtern"], ["tern_d"])
        return go

    if do_pass2:
        for dti in range(8):
            for cs in range(0, HID, 256):
                quant_steps.append(q_mean_slab(W1T, dti, cs, 0))
        for dti in range(32):
            for cs in range(0, D, 256):
                quant_steps.append(q_mean_slab(W2T, dti, cs, 1))
        quant_steps.append(q_thresh(0, D * HID))
        quant_steps.append(q_thresh(1, D * HID))
        for dti in range(8):
            for cs in range(0, HID, 256):
                quant_steps.append(q_tern_slab(W1T, TERN1, dti, cs, 0, True))
        for dti in range(32):
            for cs in range(0, D, 256):
                quant_steps.append(q_tern_slab(W2T, TERN2, dti, cs, 1, False))
    qstep_i = [0]

    def run_quant_steps(n):
        for _ in range(n):
            if qstep_i[0] < len(quant_steps):
                quant_steps[qstep_i[0]]()
                qstep_i[0] += 1

    def pass1_block(b, blk):
        t0 = blk * BLK
        if blk == 0:
            G(lambda e: e.memset(halo[:, :, :], 0.0), ["halo"], ["halo"])
        for tt in range(2):
            trow = t0 + tt * 128
            SD(lambda e, b=b, tr=trow: e.dma_start(out=xin[:, :], in_=X[b, tr:tr + 128, :]),
               ["xio"], ["xio"])
            V(lambda e: e.bn_stats(lnst[:, 0:6], xin[:, 0:512]), ["xio"], ["lnst"])
            V(lambda e: e.bn_stats(lnst[:, 6:12], xin[:, 512:1024]), ["xio"], ["lnst"])
            V(lambda e: e.bn_aggr(mvst[:, 0:2], lnst[:, 0:12]), ["lnst"], ["mvst"], selfsync=True)
            S(lambda e: e.activation(wcols[:, 0:1], mvst[:, 1:2], AF.Ln, bias=cpool[:, 1:2]),
              ["mvst", "cpool"], ["wcols"])
            S(lambda e: e.activation(wcols[:, 0:1], wcols[:, 0:1], AF.Exp, scale=-0.5),
              ["wcols"], ["wcols"])
            for hf in range(2):
                V(lambda e, h3=hf: e.tensor_scalar(out=xnorm[:, :], in0=xin[:, h3 * 512:(h3 + 1) * 512],
                                                   scalar1=mvst[:, 0:1], scalar2=wcols[:, 0:1],
                                                   op0=ALU.subtract, op1=ALU.mult),
                  ["xio", "mvst", "wcols"], ["xnorm"])
                for dt_ in range(4):
                    T(lambda e, d=dt_: e.transpose(psT[:, d * 128:(d + 1) * 128],
                                                   xnorm[:, d * 128:(d + 1) * 128], ident_b[:, :]),
                      ["xnorm", "ident_b"], [RB[4]])
                for dt_ in range(4):
                    S(lambda e, d=dt_, h3=hf, ttt=tt: e.copy(xnt[:, h3 * 4 + d, ttt * 128:(ttt + 1) * 128],
                                                             psT[:, d * 128:(d + 1) * 128]),
                      [RB[4]], ["xnt"])

        plan = []
        for i in range(8):
            plan.append((wq_s, i, qs, i, 0 + 4 * i, "conv"))
        for i in range(8):
            plan.append((wk_s, i, ks, i, 32 + 4 * i, "conv"))
        for i in range(16):
            plan.append((wv_s, i, vs, i, 64 + 4 * i, "conv"))

        silu_jobs = []
        for pi, (wsb, ci, dst, di, cvbase, kind) in enumerate(plan):
            bank = pi % 2
            ps = psC if bank == 0 else psD
            pr = RB[2 + bank]
            wname = {id(wq_s): "wq_s", id(wk_s): "wk_s", id(wv_s): "wv_s"}[id(wsb)]
            for dt_ in range(8):
                T(lambda e, d=dt_, w=wsb, c=ci, p=ps: e.matmul(
                    p[:, 0:BLK], w[:, d, c * 128:(c + 1) * 128], xnt[:, d, :],
                    start=(d == 0), stop=(d == 7)), [wname, "xnt"], [pr])
            if True:
                par = pi % 2
                zb_ = zbuf
                S(lambda e, p=ps, pa=par: e.copy(zb_[:, pa, 4:4 + BLK], p[:, 0:BLK]),
                  [pr], ["zbuf%d" % par])
                V(lambda e, pa=par, cb=cvbase // 4: e.tensor_copy(zb_[:, pa, 1:4], halo[:, cb, 0:3]),
                  ["halo", "zbuf%d" % par], ["zbuf%d" % par])
                V(lambda e, pa=par, cb=cvbase // 4: e.tensor_copy(halo[:, cb, 0:3], zb_[:, pa, 1 + BLK:4 + BLK]),
                  ["zbuf%d" % par], ["halo"])
                for j in range(4):
                    dstc = cva if j == 0 else cvb
                    dn = ("cva%d" % par) if j == 0 else ("cvb%d" % par)
                    S(lambda e, pa=par, cv=cvbase, jj=j, dc=dstc: e.activation(
                        dc[:, pa, :], zb_[:, pa, 1 + jj:1 + jj + BLK], AF.Copy,
                        scale=convw[:, cv + jj:cv + jj + 1]), ["zbuf%d" % par, "convw"], [dn])
                    if j > 0:
                        V(lambda e, pa=par: e.tensor_tensor(cva[:, pa, :], cva[:, pa, :], cvb[:, pa, :],
                                                            ALU.add), ["cva%d" % par, "cvb%d" % par], ["cva%d" % par])
                silu_jobs.append((par, dst, di, {id(qs): "qs", id(ks): "ks", id(vs): "vs"}[id(dst)], pi))
                if len(silu_jobs) == 2 or pi == len(plan) - 1:
                    for (pa2, dd2, di2, dn2, _) in silu_jobs:
                        S(lambda e, pa3=pa2, dd3=dd2, di3=di2: e.activation(dd3[:, di3, :], cva[:, pa3, :], AF.Silu),
                          ["cva%d" % pa2], [dn2])
                    silu_jobs = []

        for dt_ in range(8):
            T(lambda e, d=dt_: e.matmul(psC[0:32, 0:BLK], wa_s[:, d, :], xnt[:, d, :],
                                        start=(d == 0), stop=(d == 7)), ["wa_s", "xnt"], [RB[2]])
        S(lambda e: e.copy(zarow[:, :], psC[0:32, 0:BLK]), [RB[2]], ["zarow"])
        for dt_ in range(8):
            T(lambda e, d=dt_: e.matmul(psD[0:32, 0:BLK], wb_s[:, d, :], xnt[:, d, :],
                                        start=(d == 0), stop=(d == 7)), ["wb_s", "xnt"], [RB[3]])

        for (src, srcn, dstrow, dstn, bank) in ((qs, "qs", lia, "lia", 0), (ks, "ks", lik, "lik", 1)):
            ps = psA if bank == 0 else psB
            pr = RB[bank]
            for hh in range(8):
                S(lambda e, h2=hh, s2=src: e.activation(cvb[:, 0, :], s2[:, h2, :], AF.Square),
                  [srcn, "cvb0"], ["cvb0"])
                T(lambda e, h2=hh: e.matmul(ps[0:8, BLK:2 * BLK], selq[:, h2 * 8:(h2 + 1) * 8], cvb[:, 0, :],
                                            start=(h2 == 0), stop=(h2 == 7)), ["cvb0", "selq"], [pr])
            V(lambda e: e.tensor_scalar(out=spw0[0:8, :], in0=ps[0:8, BLK:2 * BLK], scalar1=1e-12,
                                        scalar2=None, op0=ALU.max), [pr], ["spw0"])
            S(lambda e: e.activation(spw0[0:8, :], spw0[0:8, :], AF.Ln), ["spw0"], ["spw0"])
            V(lambda e, dr=dstrow: e.tensor_scalar(out=dr[0:8, :], in0=spw0[0:8, :], scalar1=-0.5,
                                                   scalar2=None, op0=ALU.mult), ["spw0"], [dstn])

        V(lambda e: e.tensor_scalar(out=zarow[:, :], in0=zarow[:, :], scalar1=abc[:, 1:2],
                                    scalar2=None, op0=ALU.add), ["zarow", "abc"], ["zarow"])
        S(lambda e: e.activation(spw0[:, :], zarow[:, :], AF.Relu), ["zarow"], ["spw0"])
        S(lambda e: e.activation(grow[:, :], zarow[:, :], AF.Abs), ["zarow"], ["grow"])
        S(lambda e: e.activation(grow[:, :], grow[:, :], AF.Exp, scale=-1.0), ["grow"], ["grow"])
        S(lambda e: e.activation(grow[:, :], grow[:, :], AF.Ln, bias=cpool[0:32, 0:1]), ["grow", "cpool"], ["grow"])
        V(lambda e: e.tensor_tensor(spw0[:, :], spw0[:, :], grow[:, :], ALU.add), ["spw0", "grow"], ["spw0"])
        V(lambda e: e.tensor_scalar(out=grow[:, :], in0=spw0[:, :], scalar1=negexpA[:, 0:1],
                                    scalar2=None, op0=ALU.mult), ["spw0", "negexpA"], ["grow"])
        S(lambda e: e.copy(zarow[:, :], psD[0:32, 0:BLK]), [RB[3], "zarow"], ["zarow"])
        S(lambda e: e.activation(spw0[:, :], zarow[:, :], AF.Relu, scale=-1.0), ["zarow", "spw0"], ["spw0"])
        S(lambda e: e.activation(lnbrow[:, :], zarow[:, :], AF.Abs), ["zarow"], ["lnbrow"])
        S(lambda e: e.activation(lnbrow[:, :], lnbrow[:, :], AF.Exp, scale=-1.0), ["lnbrow"], ["lnbrow"])
        S(lambda e: e.activation(lnbrow[:, :], lnbrow[:, :], AF.Ln, bias=cpool[0:32, 0:1]), ["lnbrow", "cpool"], ["lnbrow"])
        V(lambda e: e.tensor_tensor(spw0[:, :], spw0[:, :], lnbrow[:, :], ALU.add), ["spw0", "lnbrow"], ["spw0"])
        V(lambda e: e.tensor_scalar(out=lnbrow[:, :], in0=spw0[:, :], scalar1=np.log(2.0), scalar2=-1.0,
                                    op0=ALU.subtract, op1=ALU.mult), ["spw0"], ["lnbrow"])

        if cfg.get("dump") and b == 0 and blk == 0:
            SD(lambda e: e.dma_start(out=DBG["qs"].ap(), in_=qs[:, :, :].rearrange("p a b -> p (a b)")), ["qs"], ["dbgd"])
            SD(lambda e: e.dma_start(out=DBG["ks"].ap(), in_=ks[:, :, :].rearrange("p a b -> p (a b)")), ["ks"], ["dbgd"])
            SD(lambda e: e.dma_start(out=DBG["vs"].ap(), in_=vs[:, :, :].rearrange("p a b -> p (a b)")), ["vs"], ["dbgd"])
            SD(lambda e: e.dma_start(out=DBG["grow"].ap(), in_=grow[:, :]), ["grow"], ["dbgd"])
            SD(lambda e: e.dma_start(out=DBG["lnbrow"].ap(), in_=lnbrow[:, :]), ["lnbrow"], ["dbgd"])
            SD(lambda e: e.dma_start(out=DBG["lia"].ap(), in_=lia[:, :]), ["lia"], ["dbgd"])
            SD(lambda e: e.dma_start(out=DBG["lik"].ap(), in_=lik[:, :]), ["lik"], ["dbgd"])
            SD(lambda e: e.dma_start(out=DBG["xnt"].ap(), in_=xnt[:, :, :].rearrange("p a b -> p (a b)")), ["xnt"], ["dbgd"])

        for c in range(2):
            cp = 0
            cs = c * 128
            rs = "rows%d" % cp
            V(lambda e, pa=cp, c0=cs: e.tensor_tensor_scan(
                rowstack[:, pa, 0:128], grow[:, c0:c0 + 128], grow[:, c0:c0 + 128], 0.0,
                ALU.add, ALU.bypass), ["grow", rs], [rs])
            V(lambda e, pa=cp, c0=cs: e.tensor_tensor(rowstack[:, pa, 128:256], rowstack[:, pa, 0:128],
                                                      lnbrow[:, c0:c0 + 128], ALU.add), [rs, "lnbrow"], [rs])
            V(lambda e, pa=cp, c0=cs: e.tensor_tensor(rowstack[:, pa, 128:256], rowstack[:, pa, 128:256],
                                                      lik[:, c0:c0 + 128], ALU.add), [rs, "lik"], [rs])
            V(lambda e, pa=cp, c0=cs: e.tensor_tensor(rowstack[:, pa, 256:384], rowstack[:, pa, 0:128],
                                                      lik[:, c0:c0 + 128], ALU.subtract), [rs, "lik"], [rs])
            V(lambda e, pa=cp, c0=cs: e.tensor_tensor(rowstack[:, pa, 384:512], rowstack[:, pa, 0:128],
                                                      lia[:, c0:c0 + 128], ALU.add), [rs, "lia"], [rs])

            for ri, off in ((1, 0), (2, 32)):
                T(lambda e, pa=cp, r2=ri, o2=off: e.transpose(
                    psW[:, o2:o2 + 32], rowstack[:, pa, r2 * 128:(r2 + 1) * 128],
                    ident_f[:, :]), [rs, "ident_f"], [RB[7]])
            T(lambda e, c0=cs: e.transpose(psW[:, 64:96], lnbrow[:, c0:c0 + 128],
                                           ident_f[:, :]), ["lnbrow", "ident_f"], [RB[7]])
            cst = "cols%d" % cp
            V(lambda e, pa=cp: e.tensor_copy(colstack[:, pa, 0:96], psW[:, 0:96]), [RB[7]], [cst])
            V(lambda e, pa=cp: e.tensor_scalar(out=nrb[:, pa, 0:8], in0=colstack[:, pa, 32:40],
                                               scalar1=-1.0, scalar2=None, op0=ALU.mult), [cst], ["nrb%d" % cp])
            S(lambda e, pa=cp: e.activation(gbh[:, pa, 0:8], colstack[:, pa, 0:8], AF.Exp), [cst], ["gbh%d" % cp])
            S(lambda e, pa=cp: e.activation(bcols[:, pa, 0:8], colstack[:, pa, 64:72], AF.Exp), [cst], ["bcol%d" % cp])

            for gi in range(16):
                bank = gi % 2
                psg = psC if bank == 0 else psD
                prg = RB[2 + bank]
                for dt_ in range(8):
                    T(lambda e, d=dt_, g3=gi, p=psg, c0=cs: e.matmul(
                        p[:, 0:128], wg_s[:, d, g3 * 128:(g3 + 1) * 128], xnt[:, d, c0:c0 + 128],
                        start=(d == 0), stop=(d == 7)), ["wg_s", "xnt"], [prg])
                S(lambda e, g3=gi, p=psg: e.activation(sgate[:, g3, :], p[:, 0:128], AF.Silu),
                  [prg], ["sgate"])
                V(lambda e, g3=gi: e.tensor_scalar(
                    out=sgate[:, g3, :], in0=sgate[:, g3, :], scalar1=onorm[:, g3 % 2:g3 % 2 + 1],
                    scalar2=None, op0=ALU.mult), ["sgate", "onorm"], ["sgate"])

            first_chunk = (blk == 0 and c == 0)
            ygates = cfg.get("ygates")

            def yg(i):
                return ygates is None or i in ygates

            def gen_group(hh):
                gp = hh % NG
                gn = "g%d" % gp
                bE = psA if gp == 0 else psU
                bO = psB if gp == 0 else psO
                bEr = RB[0] if gp == 0 else RB[5]
                bOr = RB[1] if gp == 0 else RB[6]
                tb = 512 * gp
                slot = b * 8 + hh
                for (rsl_, dst_) in ((slice(128, 256), 0), (slice(256, 384), 128), (slice(384, 512), 256)):
                    V(lambda e, pa=cp, h2=hh, g2=gp, r2=rsl_: e.tensor_scalar(
                        out=zpick[:, g2, :], in0=rowstack[:, pa, r2], scalar1=epick[:, h2:h2 + 1],
                        scalar2=None, op0=ALU.mult), [rs, "epick"], ["zp" + gn])
                    T(lambda e, g2=gp, d2=dst_: e.matmul(bE[:, d2:d2 + 128], ones32[:, :], zpick[:, g2, :],
                                                         start=True, stop=True), ["zp" + gn, "ones32"], [bEr])
                    if yg(0):
                        yield
                V(lambda e, pa=cp, h2=hh, g2=gp: e.tensor_scalar(
                    out=zpickc[:, g2, :], in0=rowstack[:, pa, 127:128], scalar1=epick[:, h2:h2 + 1],
                    scalar2=None, op0=ALU.mult), [rs, "epick"], ["zc" + gn])
                T(lambda e, g2=gp: e.matmul(bO[:, 0:1], ones32[:, :], zpickc[:, g2, :],
                                            start=True, stop=True), ["zc" + gn, "ones32"], [bOr])
                pe_fence(bO, bOr, 135)
                V(lambda e, g2=gp: e.tensor_copy(l127c[:, g2, :], bO[:, 0:1]), [bOr], ["l127" + gn])
                V(lambda e, g2=gp, h2=hh, c0=cs: e.tensor_scalar(
                    out=nkb[:, g2, :], in0=ks[:, h2, c0:c0 + 128], scalar1=-1.0, scalar2=None,
                    op0=ALU.mult), ["ks"], ["nk" + gn])
                T(lambda e, g2=gp, h2=hh, c0=cs: e.matmul(bE[:, 384:512], nkb[:, g2, :],
                                                          ks[:, h2, c0:c0 + 128], start=True, stop=True),
                  ["nk" + gn, "ks"], [bEr])
                T(lambda e, g2=gp, h2=hh, c0=cs: e.matmul(bO[:, 1:129], ks[:, h2, c0:c0 + 128],
                                                          qs[:, h2, c0:c0 + 128], start=True, stop=True),
                  ["ks", "qs"], [bOr])
                if yg(1):
                    yield
                V(lambda e, g2=gp: e.tensor_tensor(ebufb[:, g2, :], bE[:, 0:128], maskus[:, :], ALU.add),
                  [bEr, "maskus"], ["eb" + gn])
                S(lambda e, g2=gp, pa=cp, h2=hh: e.activation(ebufb[:, g2, :], ebufb[:, g2, :], AF.Exp,
                                                              bias=nrb[:, pa, h2:h2 + 1]),
                  ["eb" + gn, "nrb%d" % cp], ["eb" + gn])
                V(lambda e, g2=gp: e.tensor_tensor(nnpb[:, g2, :], ebufb[:, g2, :], bE[:, 384:512], ALU.mult),
                  ["eb" + gn, bEr], ["nnp" + gn])
                if yg(2):
                    yield
                V(lambda e, g2=gp: e.tensor_tensor(ebufb[:, g2, :], bE[:, 128:256], maskls[:, :], ALU.add),
                  [bEr, "maskls"], ["eb" + gn])
                S(lambda e, g2=gp, pa=cp, h2=hh: e.activation(ebufb[:, g2, :], ebufb[:, g2, :], AF.Exp,
                                                              scale=-1.0, bias=colstack[:, pa, h2:h2 + 1]),
                  ["eb" + gn, "cols%d" % cp], ["eb" + gn])
                V(lambda e, g2=gp: e.tensor_tensor(nnb[:, g2, :], ebufb[:, g2, :], bE[:, 384:512], ALU.mult),
                  ["eb" + gn, bEr], ["nn" + gn])
                if yg(3):
                    yield
                V(lambda e, g2=gp: e.tensor_tensor(ebufb[:, g2, :], bE[:, 256:384], maskui[:, :], ALU.add),
                  [bEr, "maskui"], ["eb" + gn])
                S(lambda e, g2=gp, pa=cp, h2=hh: e.activation(ebufb[:, g2, :], ebufb[:, g2, :], AF.Exp,
                                                              bias=nrb[:, pa, h2:h2 + 1]),
                  ["eb" + gn, "nrb%d" % cp], ["eb" + gn])
                V(lambda e, g2=gp: e.tensor_tensor(mtb[:, g2, :], ebufb[:, g2, :], bO[:, 1:129], ALU.mult),
                  ["eb" + gn, bOr], ["mt" + gn])
                if yg(4):
                    yield
                S(lambda e, g2=gp: e.activation(gbcb[:, g2, :], bE[:, 256:384], AF.Exp), [bEr], ["gbc" + gn])
                V(lambda e, g2=gp, h2=hh, c0=cs: e.tensor_tensor(rqb[:, g2, :], gbcb[:, g2, :],
                                                                 qs[:, h2, c0:c0 + 128], ALU.mult),
                  ["gbc" + gn, "qs"], ["rq" + gn])
                T(lambda e, h2=hh, c0=cs, t2=tb: e.transpose(psT[:, t2:t2 + 128], ks[:, h2, c0:c0 + 128],
                                                             ident_b[:, :]), ["ks", "ident_b"], [RB[4]])
                V(lambda e, g2=gp, t2=tb: e.tensor_copy(ktb[:, g2, :], psT[:, t2:t2 + 128]), [RB[4]], ["kt" + gn])
                V(lambda e, g2=gp, pa=cp, h2=hh, t2=tb: e.tensor_scalar(
                    out=y0b[:, g2, :], in0=psT[:, t2:t2 + 128], scalar1=gbh[:, pa, h2:h2 + 1], scalar2=None,
                    op0=ALU.mult), [RB[4], "gbh%d" % cp], ["y0" + gn])
                if yg(5):
                    yield
                for vt2 in range(2):
                    T(lambda e, h2=hh, c0=cs, v2=vt2, t2=tb: e.transpose(
                        psT[:, t2 + 128 + v2 * 128:t2 + 256 + v2 * 128], vs[:, 2 * h2 + v2, c0:c0 + 128],
                        ident_b[:, :]), ["vs", "ident_b"], [RB[4]])
                V(lambda e, g2=gp, pa=cp, h2=hh, t2=tb: e.tensor_scalar(
                    out=y1b[:, g2, :], in0=psT[:, t2 + 128:t2 + 384], scalar1=bcols[:, pa, h2:h2 + 1],
                    scalar2=None, op0=ALU.mult), [RB[4], "bcol%d" % cp], ["y1" + gn])
                if yg(6):
                    yield
                db = [(bE, bEr, 0), (bO, bOr, 129)]
                mmi = [0]

                def dmm(lhs_name, lhs_ap, rhs_name, rhs_ap, out_name, out_ap, sub_from=None):
                    bk, bkr, off = db[mmi[0] % 2]
                    mmi[0] += 1
                    T(lambda e, l2=lhs_ap, r2=rhs_ap, p=bk, o2=off: e.matmul(
                        p[:, o2:o2 + 128], l2, r2, start=True, stop=True), [lhs_name, rhs_name], [bkr])
                    if sub_from is None:
                        V(lambda e, p=bk, o3=out_ap, o2=off: e.tensor_copy(o3, p[:, o2:o2 + 128]),
                          [bkr], [out_name])
                    else:
                        sf_name, sf_ap = sub_from
                        V(lambda e, p=bk, o3=out_ap, s2=sf_ap, o2=off: e.tensor_tensor(
                            o3, s2, p[:, o2:o2 + 128], ALU.subtract), [bkr, sf_name], [out_name])

                def padd(lhs_name, lhs_ap, rhs_name, rhs_ap, out_name, out_ap):
                    bk, bkr, off = db[mmi[0] % 2]
                    mmi[0] += 1
                    T(lambda e, l2=lhs_ap, r2=rhs_ap, p=bk, o2=off: e.matmul(
                        p[:, o2:o2 + 128], l2, r2, start=True, stop=True), [lhs_name, rhs_name], [bkr])
                    V(lambda e, p=bk, o3=out_ap, s2=rhs_ap, o2=off: e.tensor_tensor(
                        o3, s2, p[:, o2:o2 + 128], ALU.add), [bkr, rhs_name], [out_name])

                rq = "rq2" + gn

                def rqa(sl):
                    return rqbuf[:, gp, sl * 128:(sl + 1) * 128]
                V(lambda e, g2=gp: e.tensor_tensor(ebufb[:, g2, :], nnpb[:, g2, :], mbd[:, :], ALU.mult),
                  ["nnp" + gn, "mbd"], ["eb" + gn])
                V(lambda e, g2=gp: e.tensor_copy(pcb[:, g2, 0, :], ebufb[:, g2, :]), ["eb" + gn], ["pc" + gn])
                V(lambda e, g2=gp: e.tensor_tensor(nkb[:, g2, :], nnb[:, g2, :], mbdT[:, :], ALU.mult),
                  ["nn" + gn, "mbdT"], ["nk" + gn])
                V(lambda e, g2=gp: e.tensor_tensor(astb[:, g2, :], nnpb[:, g2, :], msbu[:, :], ALU.mult),
                  ["nnp" + gn, "msbu"], ["ast" + gn])
                if yg(7):
                    yield
                nbdp_n, nbdp = "pc" + gn, pcb[:, gp, 0, :]
                nbdn_n, nbdn = "nk" + gn, nkb[:, gp, :]
                dmm(nbdp_n, nbdp, nbdn_n, nbdn, rq, rqa(0))
                dmm(nbdn_n, nbdn, nbdp_n, nbdp, rq, rqa(1))
                V(lambda e, g2=gp: e.tensor_tensor(pcb[:, g2, 1, :], pcb[:, g2, 0, :], ident_b[:, :],
                                                   ALU.add), ["pc" + gn, "ident_b"], ["pc" + gn])
                if yg(8):
                    yield
                padd(rq, rqa(0), "pc" + gn, pcb[:, gp, 1, :], "pc" + gn, pcb[:, gp, 0, :])
                if yg(9):
                    yield
                dmm(rq, rqa(1), rq, rqa(0), rq, rqa(2))
                if yg(10):
                    yield
                dmm(rq, rqa(0), rq, rqa(1), rq, rqa(3))
                if yg(11):
                    yield
                padd(rq, rqa(2), "pc" + gn, pcb[:, gp, 0, :], "pc" + gn, pcb[:, gp, 1, :])
                if yg(12):
                    yield
                dmm(rq, rqa(3), rq, rqa(2), rq, rqa(0))
                if yg(13):
                    yield
                padd(rq, rqa(0), "pc" + gn, pcb[:, gp, 1, :], "xbd" + gn, xbdb[:, gp, :])
                if yg(14):
                    yield
                T(lambda e, g2=gp, t2=tb: e.transpose(psT[:, t2 + 384:t2 + 512], xbdb[:, g2, :], ident_b[:, :]),
                  ["xbd" + gn, "ident_b"], [RB[4]])
                V(lambda e, g2=gp, t2=tb: e.tensor_copy(nkb[:, g2, :], psT[:, t2 + 384:t2 + 512]),
                  [RB[4]], ["nk" + gn])
                if yg(15):
                    yield
                dmm("ast" + gn, astb[:, gp, :], "nk" + gn, nkb[:, gp, :], "mlit" + gn, mlitb[:, gp, :])
                if yg(16):
                    yield
                wnames = ["xbd" + gn, "pc" + gn, "pc" + gn]
                waps = [xbdb[:, gp, :], pcb[:, gp, 0, :], pcb[:, gp, 1, :]]
                cur = 0
                for it in range(7):
                    nxt2 = 1 + (it % 2)
                    last = (it == 6)
                    out_name = ("x1" + gn) if last else wnames[nxt2]
                    out_ap = x1b[:, gp, :] if last else waps[nxt2]
                    dmm("mlit" + gn, mlitb[:, gp, :], wnames[cur], waps[cur], out_name, out_ap,
                        sub_from=("xbd" + gn, xbdb[:, gp, :]))
                    cur = nxt2
                    if yg(17):
                        yield
                T(lambda e, g2=gp: e.matmul(bE[:, 256:384], y0b[:, g2, :], x1b[:, g2, :],
                                            start=True, stop=True), ["y0" + gn, "x1" + gn], [bEr])
                V(lambda e, g2=gp: e.tensor_scalar(out=wutb[:, g2, :], in0=bE[:, 256:384], scalar1=-1.0,
                                                   scalar2=None, op0=ALU.mult), [bEr], ["wut" + gn])
                if yg(18):
                    yield
                T(lambda e, g2=gp, fc=first_chunk: e.matmul(bE[:, 0:256], x1b[:, g2, :], y1b[:, g2, :],
                                                            start=True, stop=fc), ["x1" + gn, "y1" + gn], [bEr])
                if not first_chunk:
                    T(lambda e, g2=gp, sl2=slot: e.matmul(bE[:, 0:256], wutb[:, g2, :], stb[:, sl2, :],
                                                          start=False, stop=True), ["wut" + gn, "stb"], [bEr])
                V(lambda e, g2=gp: e.tensor_copy(ubb[:, g2, :], bE[:, 0:256]), [bEr], ["ub" + gn])
                if yg(19):
                    yield
                for vt2 in range(2):
                    T(lambda e, g2=gp, v2=vt2: e.matmul(
                        bE[:, v2 * 128:(v2 + 1) * 128], ubb[:, g2, v2 * 128:(v2 + 1) * 128],
                        mtb[:, g2, :], start=True, stop=first_chunk), ["ub" + gn, "mt" + gn], [bEr])
                    if not first_chunk:
                        T(lambda e, g2=gp, v2=vt2, sl2=slot: e.matmul(
                            bE[:, v2 * 128:(v2 + 1) * 128], stb[:, sl2, v2 * 128:(v2 + 1) * 128],
                            rqb[:, g2, :], start=False, stop=True), ["stb", "rq" + gn], [bEr])
                    if yg(20):
                        yield
                if cfg.get("dump") and b == 0 and blk == 0 and c == 0 and hh == cfg.get("dump_head", 0):
                    for nm2, buf2, res2 in [("nnp", nnpb[:, gp, :], "nnp" + gn), ("nn", nnb[:, gp, :], "nn" + gn),
                                            ("mt", mtb[:, gp, :], "mt" + gn), ("x1", x1b[:, gp, :], "x1" + gn),
                                            ("y0", y0b[:, gp, :], "y0" + gn), ("y1", y1b[:, gp, :], "y1" + gn),
                                            ("ub", ubb[:, gp, :], "ub" + gn), ("gbc", gbcb[:, gp, :], "gbc" + gn)]:
                        SD(lambda e, n3=nm2, b3=buf2: e.dma_start(out=DBG[n3].ap(), in_=b3),
                           [res2], ["dbgd"])
                S(lambda e, g2=gp: e.activation(rqbuf[:, g2, 0:256], bE[:, 0:256], AF.Square),
                  [bEr, rq], [rq])
                for vt2 in range(2):
                    T(lambda e, g2=gp, v2=vt2: e.matmul(bE[0:1, 256:384], onescol[:, :],
                                                        rqbuf[:, g2, v2 * 128:(v2 + 1) * 128],
                                                        start=(v2 == 0), stop=(v2 == 1)),
                      [rq, "onescol"], [bEr])
                if yg(21):
                    yield
                pe_fence(bE, bEr, 400)
                S(lambda e, g2=gp: e.activation(rrow[0:1, g2, :], bE[0:1, 256:384], AF.Ln, scale=1.0 / VD,
                                                bias=cpool[0:1, 1:2]), [bEr, "cpool"], ["rr" + gn])
                S(lambda e, g2=gp: e.activation(rrow[0:1, g2, :], rrow[0:1, g2, :], AF.Exp, scale=-0.5),
                  ["rr" + gn], ["rr" + gn])
                V(lambda e, g2=gp: e.tensor_copy(rrowb[0:1, g2, :], rrow[0:1, g2, :]), ["rr" + gn], ["rrb" + gn])
                T(lambda e, g2=gp: e.matmul(bE[:, 384:512], ones1r[:, :], rrowb[0:1, g2, :],
                                            start=True, stop=True), ["rrb" + gn, "ones1r"], [bEr])
                if yg(22):
                    yield
                for vt2 in range(2):
                    V(lambda e, g2=gp, h2=hh, v2=vt2: e.tensor_tensor(
                        o_all[:, 0, 2 * h2 + v2, :], bE[:, v2 * 128:(v2 + 1) * 128],
                        sgate[:, 2 * h2 + v2, 0:128], ALU.mult), [bEr, "sgate"], ["oall"])
                for vt2 in range(2):
                    V(lambda e, g2=gp, h2=hh, v2=vt2: e.tensor_tensor(
                        o_all[:, 0, 2 * h2 + v2, :], o_all[:, 0, 2 * h2 + v2, :],
                        bE[:, 384:512], ALU.mult), ["oall", bEr], ["oall"])
                if yg(23):
                    yield
                S(lambda e, g2=gp, pa=cp, h2=hh: e.activation(dgc[:, g2, :], colstack[:, pa, 32 + h2:33 + h2],
                                                              AF.Exp, scale=-1.0, bias=l127c[:, g2, 0:1]),
                  ["cols%d" % cp, "l127" + gn], ["dg" + gn])
                S(lambda e, g2=gp: e.activation(gamc[:, g2, :], l127c[:, g2, 0:1], AF.Exp),
                  ["l127" + gn], ["gam" + gn])
                V(lambda e, g2=gp: e.tensor_scalar(out=dkb[:, g2, :], in0=ktb[:, g2, :], scalar1=dgc[:, g2, 0:1],
                                                   scalar2=None, op0=ALU.mult), ["kt" + gn, "dg" + gn], ["dk" + gn])
                T(lambda e, g2=gp: e.matmul(bO[:, 256:512], dkb[:, g2, :], ubb[:, g2, :],
                                            start=True, stop=True), ["dk" + gn, "ub" + gn], [bOr])
                if yg(24):
                    yield
                if first_chunk:
                    V(lambda e, sl2=slot, p=bO: e.tensor_copy(stb[:, sl2, :], p[:, 256:512]),
                      [bOr], ["stb"])
                else:
                    V(lambda e, g2=gp, sl2=slot: e.tensor_scalar(
                        out=stb[:, sl2, :], in0=stb[:, sl2, :], scalar1=gamc[:, g2, 0:1], scalar2=None,
                        op0=ALU.mult), ["stb", "gam" + gn], ["stb"])
                    V(lambda e, sl2=slot, p=bO: e.tensor_tensor(
                        stb[:, sl2, :], stb[:, sl2, :], p[:, 256:512], ALU.add),
                      ["stb", bOr], ["stb"])

            gens = [gen_group(hh) for hh in range(H)]
            if cfg.get("nointl"):
                for g in gens:
                    for _ in g:
                        pass
                gens = []
            active = [g for g in gens[0:2]]
            nxt_h = 2
            while active:
                i = 0
                while i < len(active):
                    g = active[i]
                    try:
                        next(g)
                        i += 1
                    except StopIteration:
                        active.pop(i)
                        if nxt_h < H:
                            active.append(gens[nxt_h])
                            nxt_h += 1

            if cfg.get("dump") and b == 0 and blk == 0 and c == 0:
                SD(lambda e: e.dma_start(out=DBG["sgate"].ap(), in_=sgate[:, :, :].rearrange("p a b -> p (a b)")), ["sgate"], ["dbgd"])
                SD(lambda e: e.dma_start(out=DBG["rowstack"].ap(), in_=rowstack[:, 0, :]), ["rows0"], ["dbgd"])
                SD(lambda e: e.dma_start(out=DBG["oall"].ap(), in_=o_all[:, 0, :, :].rearrange("p a b -> p (a b)")), ["oall"], ["dbgd"])
                SD(lambda e: e.dma_start(out=DBG["stb"].ap(), in_=stb[:, :, :].rearrange("p a b -> p (a b)")), ["stb"], ["dbgd"])

            for dh in range(2):
                for vt2 in range(16):
                    T(lambda e, v2=vt2, d2=dh, cc=c: e.matmul(
                        psW[:, 0:512], o_all[:, 0, v2, :], wo_s[:, v2, d2 * 512:(d2 + 1) * 512],
                        start=(v2 == 0), stop=(v2 == 15)), ["oall", "wo_s"], [RB[7]])
                V(lambda e, d2=dh: e.tensor_copy(aw[:, d2 * 512:(d2 + 1) * 512], psW[:, 0:512]),
                  [RB[7]], ["aw"])
            trow = t0 + c * 128
            SD(lambda e, b=b, tr=trow: e.dma_start(out=xres[:, :], in_=X[b, tr:tr + 128, :]),
               ["xio"], ["xio"])
            V(lambda e: e.bn_stats(lnst[:, 0:6], aw[:, 0:512]), ["aw"], ["lnst"])
            V(lambda e: e.bn_stats(lnst[:, 6:12], aw[:, 512:1024]), ["aw"], ["lnst"])
            V(lambda e: e.bn_aggr(mvst[:, 0:2], lnst[:, 0:12]), ["lnst"], ["mvst"], selfsync=True)
            S(lambda e: e.activation(wcols[:, 0:1], mvst[:, 1:2], AF.Ln, bias=cpool[:, 1:2]),
              ["mvst", "cpool"], ["wcols"])
            S(lambda e: e.activation(wcols[:, 0:1], wcols[:, 0:1], AF.Exp, scale=-0.5),
              ["wcols"], ["wcols"])
            V(lambda e: e.tensor_scalar(out=aw[:, :], in0=aw[:, :], scalar1=mvst[:, 0:1],
                                        scalar2=wcols[:, 0:1], op0=ALU.subtract, op1=ALU.mult),
              ["aw", "mvst", "wcols"], ["aw"])
            if cfg.get("dump") and b == 0 and blk == 0 and c == 0:
                SD(lambda e: e.dma_start(out=DBG["aw"].ap(), in_=aw[:, :]), ["aw"], ["dbgd"])
            G(lambda e: e.tensor_tensor(xres[:, :], xres[:, :], aw[:, :], ALU.add), ["xio", "aw"], ["xio"])
            SD(lambda e, b=b, tr=trow: e.dma_start(out=XP[b, tr:tr + 128, :], in_=xres[:, :]),
               ["xio"], ["xp_d"])

    def build_pass2_prep(p2):
        def sb2(name, shape, dt):
            return p2.enter_context(nc.sbuf_tensor(name, shape, dt))
        g1bc = sb2("g1bc", [128, D], F32)
        b1bc = sb2("b1bc", [128, D], F32)
        b2cb = sb2("b2cb", [128, D], F32)
        cgbc = sb2("cgbc", [128, D], F32)
        gb2s = sb2("gb2s", [128, 64], BF16)
        b1col = sb2("b1col", [128, 32], F32)
        g2m = sb2("g2m", [128, 32], F32)
        G1 = sb2("G1", [128, 8, HID], BF16)
        G2 = sb2("G2", [128, 32, 4, 257], BF16)
        xp2 = sb2("xp2", [128, 2, D], F32)
        xn2 = sb2("xn2", [128, D], BF16)
        xn2t = sb2("xn2t", [128, 8, BLK], BF16)
        h_all = sb2("h_all", [128, 32, BLK], BF16)
        h2t = sb2("h2t", [128, 2, 128], BF16)
        fw = sb2("fw", [128, D], F32)
        fw2 = sb2("fw2", [128, D], F32)
        cgcbb = sb2("cgcbb", [1, 2 * D], BF16)
        return dict(g1bc=g1bc, b1bc=b1bc, b2cb=b2cb, cgbc=cgbc, gb2s=gb2s, b1col=b1col,
                    g2m=g2m, G1=G1, G2=G2, xp2=xp2, xn2=xn2, xn2t=xn2t, h_all=h_all,
                    h2t=h2t, fw=fw, fw2=fw2, cgcbb=cgcbb)

    def pass2_prep(B2):
        SD(lambda e: e.dma_start(out=B2["g1bc"][:, :], in_=CG1.ap()), [], ["g1bc"])
        SD(lambda e: e.dma_start(out=B2["b1bc"][:, :], in_=CB1.ap()), [], ["b1bc"])
        SD(lambda e: e.dma_start(out=B2["b2cb"][:, :], in_=CB2.ap()), [], ["b2cb"])
        GD(lambda e: e.dma_start(out=B2["gb2s"][:, :], in_=CGB2.ap()), [], ["gb2s"])
        SD(lambda e: e.dma_start(out=B2["b1col"][:, :], in_=CB1COL.ap()), [], ["b1col"])
        SD(lambda e: e.dma_start(out=B2["g2m"][:, :], in_=CG2COL.ap()), [], ["g2m"])
        for dt_ in range(8):
            SD(lambda e, d=dt_: e.dma_start(out=B2["G1"][:, d, :], in_=TERN1[d * 128:(d + 1) * 128, :]),
               ["tern_d"], ["G1"])
        for ht in range(32):
            SD(lambda e, h2=ht: e.dma_start(
                out=B2["G2"][:, h2, :, 0:256],
                in_=TERN2[h2 * 128:(h2 + 1) * 128, :].rearrange("p (a b) -> p a b", b=256)),
               ["tern_d"], ["G2"])
        V(lambda e: e.memset(B2["G2"][:, :, :, 256:257], 1.0), ["G2"], ["G2"])
        for which, outname in ((0, "cg"), (1, "cb")):
            for pair in range(2):
                for sub in range(2):
                    gr = 2 * pair + sub
                    for ht in range(32):
                        T(lambda e, h2=ht, g2=gr, s2=sub, w2=which: e.matmul(
                            psW[0:1, s2 * 256:(s2 + 1) * 256], B2["gb2s"][:, 2 * h2 + w2:2 * h2 + w2 + 1],
                            B2["G2"][:, h2, g2, 0:256],
                            start=(h2 == 0), stop=(h2 == 31)), ["G2", "gb2s"], [RB[7]])
                V(lambda e, p2=pair, w2=which: e.tensor_copy(
                    B2["cgcbb"][0:1, w2 * D + p2 * 512:w2 * D + (p2 + 1) * 512], psW[0:1, 0:512]),
                  [RB[7]], ["cgcbb"])
        for hf in range(2):
            T(lambda e, hf2=hf: e.matmul(psW[:, 0:512], ones1r[:, :], B2["cgcbb"][0:1, hf2 * 512:(hf2 + 1) * 512],
                                         start=True, stop=True), ["cgcbb", "ones1r"], [RB[7]])
            V(lambda e, hf2=hf: e.tensor_scalar(out=B2["cgbc"][:, hf2 * 512:(hf2 + 1) * 512], in0=psW[:, 0:512],
                                                scalar1=thr[:, 5:6], scalar2=None, op0=ALU.mult),
              [RB[7], "thr"], ["cgbc"])
            T(lambda e, hf2=hf: e.matmul(psW[:, 0:512], ones1r[:, :], B2["cgcbb"][0:1, D + hf2 * 512:D + (hf2 + 1) * 512],
                                         start=True, stop=True), ["cgcbb", "ones1r"], [RB[7]])
            V(lambda e, hf2=hf: e.tensor_scalar(out=B2["fw"][:, hf2 * 512:(hf2 + 1) * 512], in0=psW[:, 0:512],
                                                scalar1=thr[:, 5:6], scalar2=None, op0=ALU.mult),
              [RB[7], "thr"], ["fw"])
        V(lambda e: e.tensor_tensor(B2["b2cb"][:, :], B2["b2cb"][:, :], B2["fw"][:, :], ALU.add),
          ["b2cb", "fw"], ["b2cb"])
        for ht in range(32):
            for gr in range(4):
                V(lambda e, h2=ht, g2=gr: e.tensor_scalar(
                    out=B2["G2"][:, h2, g2, 0:256], in0=B2["G2"][:, h2, g2, 0:256],
                    scalar1=B2["g2m"][:, h2:h2 + 1], scalar2=None, op0=ALU.mult), ["G2", "g2m"], ["G2"],
                  selfsync=True)

    def pass2_block(B2, b, blk):
        t0 = blk * BLK
        for tt in range(2):
            trow = t0 + tt * 128
            SD(lambda e, b2=b, tr=trow, t2=tt: e.dma_start(out=B2["xp2"][:, t2, :], in_=XP[b2, tr:tr + 128, :]),
               ["xp2%d" % tt], ["xp2%d" % tt])
            V(lambda e, t2=tt: e.bn_stats(lnst[:, 0:6], B2["xp2"][:, t2, 0:512]), ["xp2%d" % tt], ["lnst"])
            V(lambda e, t2=tt: e.bn_stats(lnst[:, 6:12], B2["xp2"][:, t2, 512:1024]), ["xp2%d" % tt], ["lnst"])
            V(lambda e: e.bn_aggr(mvst[:, 0:2], lnst[:, 0:12]), ["lnst"], ["mvst"], selfsync=True)
            S(lambda e: e.activation(wcols[:, 0:1], mvst[:, 1:2], AF.Ln, bias=cpool[:, 1:2]),
              ["mvst", "cpool"], ["wcols"])
            S(lambda e: e.activation(wcols[:, 0:1], wcols[:, 0:1], AF.Exp, scale=-0.5), ["wcols"], ["wcols"])
            V(lambda e: e.tensor_scalar(out=wcols[:, 1:2], in0=mvst[:, 1:2], scalar1=wcols[:, 0:1],
                                        scalar2=wcols[:, 0:1], op0=ALU.mult, op1=ALU.mult),
              ["mvst", "wcols"], ["wcols"])
            S(lambda e: e.activation(wcols[:, 1:2], wcols[:, 1:2], AF.Ln, bias=cpool[:, 1:2]),
              ["wcols", "cpool"], ["wcols"])
            S(lambda e: e.activation(wcols[:, 1:2], wcols[:, 1:2], AF.Exp, scale=-0.5), ["wcols"], ["wcols"])
            V(lambda e: e.tensor_scalar(out=wcols[:, 2:3], in0=wcols[:, 0:1], scalar1=wcols[:, 1:2],
                                        scalar2=None, op0=ALU.mult), ["wcols"], ["wcols"])
            V(lambda e, t2=tt: e.tensor_scalar(out=B2["fw"][:, :], in0=B2["xp2"][:, t2, :], scalar1=mvst[:, 0:1],
                                        scalar2=wcols[:, 2:3], op0=ALU.subtract, op1=ALU.mult),
              ["xp2%d" % tt, "mvst", "wcols"], ["fw"])
            V(lambda e: e.tensor_tensor(B2["fw"][:, :], B2["fw"][:, :], B2["g1bc"][:, :], ALU.mult),
              ["fw", "g1bc"], ["fw"])
            G(lambda e: e.tensor_tensor(B2["xn2"][:, :], B2["fw"][:, :], B2["b1bc"][:, :], ALU.add),
              ["fw", "b1bc"], ["xn2"])
            for dt_ in range(8):
                T(lambda e, d=dt_: e.transpose(psT[:, d * 128:(d + 1) * 128],
                                               B2["xn2"][:, d * 128:(d + 1) * 128], ident_b[:, :]),
                  ["xn2", "ident_b"], [RB[4]])
            for dt_ in range(8):
                S(lambda e, d=dt_, t2=tt: e.copy(B2["xn2t"][:, d, t2 * 128:(t2 + 1) * 128],
                                                 psT[:, d * 128:(d + 1) * 128]), [RB[4]], ["xn2t"])
        if cfg.get("dump") and b == 0 and blk == 0:
            SD(lambda e: e.dma_start(out=DBG["thr"].ap(), in_=thr[:, :]), ["thr"], ["dbgd"])
            SD(lambda e: e.dma_start(out=DBG["xn2"].ap(), in_=B2["xn2"][:, :]), ["xn2"], ["dbgd"])
            SD(lambda e: e.dma_start(out=DBG["cgbc"].ap(), in_=B2["cgbc"][:, :]), ["cgbc"], ["dbgd"])
            SD(lambda e: e.dma_start(out=DBG["b2cb"].ap(), in_=B2["b2cb"][:, :]), ["b2cb"], ["dbgd"])
            SD(lambda e: e.dma_start(out=DBG["G1s"].ap(), in_=B2["G1"][:, 0, :]), ["G1"], ["dbgd"])
            SD(lambda e: e.dma_start(out=DBG["G2s"].ap(), in_=B2["G2"][:, 0, :, :].rearrange("p a b -> p (a b)")), ["G2"], ["dbgd"])
        for ht in range(32):
            bank = ht % 2
            ps = psC if bank == 0 else psD
            pr = RB[2 + bank]
            for dt_ in range(8):
                T(lambda e, d=dt_, h2=ht, p=ps: e.matmul(
                    p[:, 0:BLK], B2["G1"][:, d, h2 * 128:(h2 + 1) * 128], B2["xn2t"][:, d, :],
                    start=(d == 0), stop=(d == 7)), ["G1", "xn2t"], [pr])
            S(lambda e, h2=ht, p=ps: e.activation(B2["h_all"][:, h2, :], p[:, 0:BLK], AF.Gelu,
                                                  bias=B2["b1col"][:, h2:h2 + 1]),
              [pr, "b1col"], ["h_all"])
            for tt in range(2):
                S(lambda e, h2=ht, t2=tt: e.activation(B2["h2t"][:, t2, :], B2["h_all"][:, h2, t2 * 128:(t2 + 1) * 128],
                                                       AF.Square), ["h_all"], ["h2t%d" % tt], selfsync=True)
                pss = psU if tt == 0 else psO
                prr = RB[5] if tt == 0 else RB[6]
                T(lambda e, h2=ht, t2=tt, p=pss: e.matmul(
                    p[:, 384:385], B2["h2t"][:, t2, :], onescol[:, :],
                    start=(h2 == 0), stop=(h2 == 31)), ["h2t%d" % tt, "onescol"], [prr])
        pe_fence(psU, RB[5], 386)
        pe_fence(psO, RB[6], 386)
        if cfg.get("dump") and b == 0 and blk == 0:
            V(lambda e: e.tensor_copy(mvst[:, 0:1], psU[:, 384:385]), [RB[5]], ["mvst"])
            V(lambda e: e.tensor_copy(mvst[:, 1:2], psO[:, 384:385]), [RB[6]], ["mvst"])
            SD(lambda e: e.dma_start(out=DBG["sums"].ap(), in_=mvst[:, 0:2]), ["mvst"], ["dbgd"])
        for tt in range(2):
            pss = psU if tt == 0 else psO
            prr = RB[5] if tt == 0 else RB[6]
            for gr in range(4):
                bank = gr % 2
                ps = psA if bank == 0 else psB
                pr = RB[bank]
                for ht in range(32):
                    T(lambda e, h2=ht, g2=gr, t2=tt, p=ps: e.matmul(
                        p[:, 0:257], B2["h_all"][:, h2, t2 * 128:(t2 + 1) * 128], B2["G2"][:, h2, g2, :],
                        start=(h2 == 0), stop=(h2 == 31)), ["h_all", "G2"], [pr])
                if gr == 0:
                    pe_fence(ps, pr, 300)
                    V(lambda e, p=ps: e.tensor_scalar(out=wcols[:, 3:4], in0=p[:, 256:257],
                                                      scalar1=1.0 / HID, scalar2=None, op0=ALU.mult),
                      [pr], ["wcols"])
                    V(lambda e, p=pss: e.tensor_scalar(out=wcols[:, 4:5], in0=p[:, 384:385],
                                                       scalar1=1.0 / HID, scalar2=None, op0=ALU.mult),
                      [prr], ["wcols"])
                    V(lambda e: e.tensor_scalar(out=wcols[:, 5:6], in0=wcols[:, 3:4], scalar1=wcols[:, 3:4],
                                                scalar2=None, op0=ALU.mult), ["wcols"], ["wcols"], selfsync=True)
                    V(lambda e: e.tensor_tensor(wcols[:, 4:5], wcols[:, 4:5], wcols[:, 5:6], ALU.subtract),
                      ["wcols"], ["wcols"])
                    S(lambda e: e.activation(wcols[:, 4:5], wcols[:, 4:5], AF.Ln, bias=cpool[:, 1:2]),
                      ["wcols", "cpool"], ["wcols"])
                    S(lambda e: e.activation(wcols[:, 4:5], wcols[:, 4:5], AF.Exp, scale=-0.5),
                      ["wcols"], ["wcols"])
                    V(lambda e: e.tensor_scalar(out=wcols[:, 6:7], in0=wcols[:, 4:5], scalar1=thr[:, 5:6],
                                                scalar2=None, op0=ALU.mult), ["wcols", "thr"], ["wcols"])
                    V(lambda e: e.tensor_scalar(out=wcols[:, 7:8], in0=wcols[:, 4:5], scalar1=wcols[:, 3:4],
                                                scalar2=None, op0=ALU.mult), ["wcols"], ["wcols"], selfsync=True)
                V(lambda e, g2=gr, p=ps: e.tensor_scalar(
                    out=B2["fw"][:, g2 * 256:(g2 + 1) * 256], in0=p[:, 0:256], scalar1=wcols[:, 6:7],
                    scalar2=None, op0=ALU.mult), [pr, "wcols"], ["fw"], selfsync=True)
            V(lambda e: e.tensor_scalar(out=B2["fw2"][:, :], in0=B2["cgbc"][:, :], scalar1=wcols[:, 7:8],
                                        scalar2=None, op0=ALU.mult), ["cgbc", "wcols"], ["fw2"], selfsync=True)
            G(lambda e: e.tensor_tensor(B2["fw"][:, :], B2["fw"][:, :], B2["fw2"][:, :], ALU.subtract),
              ["fw", "fw2"], ["fw"])
            G(lambda e: e.tensor_tensor(B2["fw"][:, :], B2["fw"][:, :], B2["b2cb"][:, :], ALU.add),
              ["fw", "b2cb"], ["fw"])
            if cfg.get("dump") and b == 0 and blk == 0 and tt == 0:
                SD(lambda e: e.dma_start(out=DBG["fwpre"].ap(), in_=B2["fw"][:, :]), ["fw"], ["dbgd"])
                SD(lambda e: e.dma_start(out=DBG["wcolsd"].ap(), in_=wcols[:, :]), ["wcols"], ["dbgd"])
                SD(lambda e: e.dma_start(out=DBG["hall"].ap(), in_=B2["h_all"][:, :, :].rearrange("p a b -> p (a b)")), ["h_all"], ["dbgd"])
            V(lambda e: e.bn_stats(lnst[:, 0:6], B2["fw"][:, 0:512]), ["fw"], ["lnst"])
            V(lambda e: e.bn_stats(lnst[:, 6:12], B2["fw"][:, 512:1024]), ["fw"], ["lnst"])
            V(lambda e: e.bn_aggr(mvst[:, 0:2], lnst[:, 0:12]), ["lnst"], ["mvst"], selfsync=True)
            S(lambda e: e.activation(wcols[:, 0:1], mvst[:, 1:2], AF.Ln, bias=cpool[:, 1:2]),
              ["mvst", "cpool"], ["wcols"])
            S(lambda e: e.activation(wcols[:, 0:1], wcols[:, 0:1], AF.Exp, scale=-0.5), ["wcols"], ["wcols"])
            V(lambda e: e.tensor_scalar(out=B2["fw"][:, :], in0=B2["fw"][:, :], scalar1=mvst[:, 0:1],
                                        scalar2=wcols[:, 0:1], op0=ALU.subtract, op1=ALU.mult),
              ["fw", "mvst", "wcols"], ["fw"])
            G(lambda e, t2=tt: e.tensor_tensor(B2["fw"][:, :], B2["fw"][:, :], B2["xp2"][:, t2, :], ALU.add),
              ["fw", "xp2%d" % tt], ["fw"])
            trow = t0 + tt * 128
            SD(lambda e, b2=b, tr=trow: e.dma_start(out=OUT[b2, tr:tr + 128, :], in_=B2["fw"][:, :]),
               ["fw"], ["out_d"])

    for b in range(nseq):
        for blk in range(nblk):
            pass1_block(b, blk)
            run_quant_steps(33)
    run_quant_steps(len(quant_steps))
    if do_pass2:
        em.barrier()
        p1.close()
        p2 = contextlib.ExitStack()
        B2 = build_pass2_prep(p2)
        em.paranoid = cfg.get("paranoid2", True)
        pass2_prep(B2)
        for b in range(nseq):
            for blk in range(nblk):
                pass2_block(B2, b, blk)
        em.paranoid = False
        p2.close()
    else:
        p1.close()
    em.barrier()
    em.finalize(nc)
    st.close()
    return nc



def _consts():
    i128 = np.eye(128, dtype=np.float32)
    sl = np.arange(128)
    maskus = np.where(sl[None, :] > sl[:, None], 0.0, -1e30).astype(np.float32)
    maskls = np.where(sl[None, :] < sl[:, None], 0.0, 1e30).astype(np.float32)
    maskui = np.where(sl[None, :] >= sl[:, None], 0.0, -1e30).astype(np.float32)
    ones32 = np.ones((32, 128), np.float32)
    epick = np.zeros((32, 8), np.float32)
    for h in range(8):
        epick[h, h] = 1.0
    selq = np.zeros((128, 64), np.float32)
    for h in range(8):
        selq[:, 8 * h + h] = 1.0
    ones1r = np.ones((1, 128), np.float32)
    onescol = np.ones((128, 1), np.float32)
    cpool = np.zeros((128, 8), np.float32)
    cpool[:, 0] = 1.0
    cpool[:, 1] = 1e-5
    cpool[:, 2] = 1e-12
    cpool[:, 3] = np.log(2.0)
    blkid = np.arange(128) // 16
    mbd = (blkid[:, None] == blkid[None, :]).astype(np.float32)
    msbu = -(blkid[None, :] > blkid[:, None]).astype(np.float32)
    return dict(c_mbd=mbd, c_msbu=msbu, c_ident=i128, c_maskus=maskus, c_maskls=maskls, c_maskui=maskui,
                c_ones32=ones32, c_epick=epick, c_selq=selq, c_ones1r=ones1r,
                c_onescol=onescol, c_cpool=cpool)


def make_inmap(inp, b0, nseq, nblk):
    Tloc = nblk * BLK
    cons = _consts()
    convw = np.zeros((128, 128), np.float32)
    cq = inp["L0_attn_conv_q"].reshape(8, 128, 4)
    ck = inp["L0_attn_conv_k"].reshape(8, 128, 4)
    cv = inp["L0_attn_conv_v"].reshape(16, 128, 4)
    for i in range(8):
        convw[:, 4 * i:4 * i + 4] = cq[i]
        convw[:, 32 + 4 * i:32 + 4 * i + 4] = ck[i]
    for i in range(16):
        convw[:, 64 + 4 * i:64 + 4 * i + 4] = cv[i]
    wa = np.zeros((1024, 32), np.float32)
    wa[:, 0:8] = inp["L0_attn_Wa"]
    wb = np.zeros((1024, 32), np.float32)
    wb[:, 0:8] = inp["L0_attn_Wb"]
    abc = np.zeros((32, 2), np.float32)
    abc[0:8, 0] = inp["L0_attn_A_log"]
    abc[0:8, 1] = inp["L0_attn_dt_bias"]
    onorm = np.ascontiguousarray(inp["L0_attn_o_norm_w"].reshape(2, 128).T)
    g1bc = np.ascontiguousarray(np.broadcast_to(inp["L0_ffn_ln1_g"], (128, D)))
    b1bc = np.ascontiguousarray(np.broadcast_to(inp["L0_ffn_ln1_b"], (128, D)))
    b2bc = np.ascontiguousarray(np.broadcast_to(inp["L0_ffn_b2"], (128, D)))
    gb2 = np.ascontiguousarray(np.stack([inp["L0_ffn_ln2_g"], inp["L0_ffn_ln2_b"]], axis=1).reshape(32, 128, 2).transpose(1, 0, 2).reshape(128, 64))
    b1col = np.ascontiguousarray(inp["L0_ffn_b1"].reshape(32, 128).T)
    g2col = np.ascontiguousarray(inp["L0_ffn_ln2_g"].reshape(32, 128).T)
    m = dict(
        x=np.ascontiguousarray(inp["L0_attn_x"][b0:b0 + nseq, :Tloc]),
        wq=inp["L0_attn_Wq"], wk=inp["L0_attn_Wk"], wv=inp["L0_attn_Wv"],
        wg=inp["L0_attn_Wg"], wa=wa, wb=wb, wo=inp["L0_attn_Wo"],
        w1t=np.ascontiguousarray(inp["L0_ffn_w1"].T),
        w2t=np.ascontiguousarray(inp["L0_ffn_w2"].T),
        c_convw=convw, c_abc=abc, c_onorm=onorm,
        c_g1bc=g1bc, c_b1bc=b1bc, c_b2bc=b2bc, c_gb2=gb2,
        c_b1col=b1col, c_g2col=g2col,
    )
    m.update(cons)
    return m


def kernel(**inp):
    cfg = dict(nseq=2, nblk=8, do_pass2=True)
    nc = build_nc(cfg)
    in_maps = [make_inmap(inp, 2 * c, cfg["nseq"], cfg["nblk"]) for c in range(NCORES)]
    res = run_bass_kernel_spmd(nc, in_maps, core_ids=list(range(NCORES)))
    out = np.concatenate([r["out"] for r in res.results], axis=0)
    return out.astype(np.float32)


# version 04
# speedup vs baseline: 1.0016x; 1.0016x over previous
import contextlib
import numpy as np
import concourse.bass as bass
import concourse.mybir as mybir
from concourse.bass_utils import run_bass_kernel_spmd

F32 = mybir.dt.float32
BF16 = mybir.dt.bfloat16
AF = mybir.ActivationFunctionType
ALU = mybir.AluOpType
AX = mybir.AxisListType

H, KD, VD, D, HID, CH, BLK, KS = 8, 128, 256, 1024, 4096, 128, 256, 4
NCORES = 8

QC = {("tensor", False): "t", ("vector", False): "v", ("scalar", False): "s",
      ("gpsimd", False): "g", ("gpsimd", True): "gd", ("sync", True): "sd"}
QUEUES = ["tensor", "vector", "scalar", "gpsimd", "sync"]
DMACTRS = ("sd", "gd")


class Em:

    def __init__(self):
        self.paranoid = False
        self.q = {qn: [] for qn in QUEUES}
        self.cnt = {c: 0 for c in QC.values()}
        self.lw = {}
        self.rd = {}

    def emit(self, queue, fn, reads=(), writes=(), dma=False, selfsync=False):
        ctr = QC[(queue, dma)]
        step = 16 if ctr in DMACTRS else 1
        pos = self.cnt[ctr] + step
        self.cnt[ctr] = pos
        waits = {}

        if self.paranoid:
            selfsync = True

        def addw(c, p):
            if (c != ctr or selfsync) and waits.get(c, 0) < p:
                waits[c] = p

        for r in reads:
            w = self.lw.get(r)
            if w:
                addw(*w)
        for r in writes:
            w = self.lw.get(r)
            if w:
                addw(*w)
            for (c, p) in self.rd.get(r, ()):
                addw(c, p)
        self.q[queue].append(dict(fn=fn, waits=waits, ctr=ctr, pos=pos))
        for r in reads:
            self.rd.setdefault(r, []).append((ctr, pos))
        for r in writes:
            self.lw[r] = (ctr, pos)
            self.rd[r] = []

    def barrier(self):
        snap = dict(self.cnt)
        for qn in QUEUES:
            for ctr, pos in snap.items():
                if pos > 0:
                    self.q[qn].append(dict(fn=None, waits={ctr: pos}, ctr=None, pos=None))
        self.lw = {}
        self.rd = {}

    def finalize(self, nc):
        waited = {c: set() for c in QC.values()}
        for qn in QUEUES:
            for rec in self.q[qn]:
                for c, p in rec["waits"].items():
                    waited[c].add(p)
        rank = {}
        for c in ("t", "v", "s", "g"):
            rank[c] = {p: i + 1 for i, p in enumerate(sorted(waited[c]))}
        sems = {}
        ctxs = []
        for c in QC.values():
            cm = nc.semaphore(f"sem_{c}")
            ctxs.append(cm)
            sems[c] = cm.__enter__()

        def body_for(qn):
            def body(eng):
                lastw = {}
                for rec in self.q[qn]:
                    for c, p in sorted(rec["waits"].items()):
                        val = p if c in DMACTRS else rank[c][p]
                        if lastw.get(c, -1) < val:
                            eng.wait_ge(sems[c], val)
                            lastw[c] = val
                    if rec["fn"] is None:
                        continue
                    inst = rec["fn"](eng)
                    c = rec["ctr"]
                    if c in DMACTRS:
                        inst.then_inc(sems[c], 16)
                    elif rec["pos"] in waited[c]:
                        inst.then_inc(sems[c], 1)
            return body

        with nc.Block() as block:
            block.tensor(body_for("tensor"))
            block.vector(body_for("vector"))
            block.scalar(body_for("scalar"))
            block.gpsimd(body_for("gpsimd"))
            block.sync(body_for("sync"))
        for cm in reversed(ctxs):
            cm.__exit__(None, None, None)


def build_nc(cfg):
    nseq = cfg["nseq"]
    nblk = cfg["nblk"]
    do_pass2 = cfg["do_pass2"]
    Tloc = nblk * BLK
    nc = bass.Bass("TRN2", target_bir_lowering=False)
    em = Em()
    st = contextlib.ExitStack()

    def din(name, shape, dt=F32):
        return nc.dram_tensor(name, shape, dt, kind="ExternalInput")

    X = din("x", [nseq, Tloc, D])
    WQ = din("wq", [D, D])
    WK = din("wk", [D, D])
    WV = din("wv", [D, 2048])
    WG = din("wg", [D, 2048])
    WA = din("wa", [D, 32])
    WB = din("wb", [D, 32])
    WO = din("wo", [2048, D])
    W1T = din("w1t", [D, HID])
    W2T = din("w2t", [HID, D])
    CIDENT = din("c_ident", [128, 128])
    CMASK3 = din("c_mask3", [128, 384])
    CMBD = din("c_mbd", [128, 128])
    CMSBU = din("c_msbu", [128, 128])
    CONES32 = din("c_ones32", [32, 128])
    CEPICK = din("c_epick", [32, 8])
    CSELQ = din("c_selq", [128, 64])
    CONES1R = din("c_ones1r", [1, 128])
    CONESCOL = din("c_onescol", [128, 1])
    CCPOOL = din("c_cpool", [128, 8])
    CCONVW = din("c_convw", [128, 128])
    CONORM = din("c_onorm", [128, 2])
    CABC = din("c_abc", [32, 2])
    CG1 = din("c_g1bc", [128, D])
    CB1 = din("c_b1bc", [128, D])
    CB2 = din("c_b2bc", [128, D])
    CGB2 = din("c_gb2", [128, 64])
    CB1COL = din("c_b1col", [128, 32])
    CG2COL = din("c_g2col", [128, 32])

    OUT = nc.dram_tensor("out", [nseq, Tloc, D], F32, kind="ExternalOutput")
    DBG = {}
    if cfg.get("dump"):
        for nm, shp, dt_ in [("nnp", [128, 128], BF16), ("nn", [128, 128], BF16),
                             ("mt", [128, 128], BF16), ("x1", [128, 128], BF16),
                             ("y0", [128, 128], BF16), ("y1", [128, VD], BF16),
                             ("ub", [128, VD], BF16), ("gbc", [128, 128], BF16),
                             ("thr", [128, 8], F32), ("xn2", [128, D], BF16),
                             ("wcolsd", [128, 8], F32), ("sums", [128, 2], F32),
                             ("hall", [128, 32 * BLK], BF16), ("fwpre", [128, D], F32),
                             ("cgbc", [128, D], F32), ("b2cb", [128, D], F32),
                             ("G1s", [128, HID], BF16), ("G2s", [128, 4 * 257], BF16),
                             ("qs", [128, 8 * BLK], BF16), ("ks", [128, 8 * BLK], BF16),
                             ("vs", [128, 16 * BLK], BF16), ("sgate", [128, 16 * 128], BF16),
                             ("grow", [32, BLK], F32), ("lnbrow", [32, BLK], F32),
                             ("lia", [32, BLK], F32), ("lik", [32, BLK], F32),
                             ("rowstack", [32, 512], F32), ("oall", [128, 16 * 128], BF16),
                             ("aw", [128, D], F32), ("stb", [128, 16 * VD], BF16),
                             ("xnt", [128, 8 * BLK], BF16)]:
            DBG[nm] = nc.dram_tensor("dbg_" + nm, shp, dt_, kind="ExternalOutput")
    XP = nc.dram_tensor("xprime", [nseq, Tloc, D], F32,
                        kind="Internal" if do_pass2 else "ExternalOutput")
    TERN1 = nc.dram_tensor("tern1", [D, HID], BF16, kind="Internal")
    TERN2 = nc.dram_tensor("tern2", [HID, D], BF16, kind="Internal")


    def sb(name, shape, dt):
        return st.enter_context(nc.sbuf_tensor(name, shape, dt))

    ident_b = sb("ident_b", [128, 128], BF16)
    ident_f = sb("ident_f", [32, 32], F32)
    mask3 = sb("mask3", [128, 384], BF16)
    mbd = sb("mbd", [128, 128], BF16)
    mbdT = sb("mbdT", [128, 128], BF16)
    msbu = sb("msbu", [128, 128], BF16)
    ones32 = sb("ones32", [32, 128], F32)
    epick = sb("epick", [32, 8], F32)
    selq = sb("selq", [128, 64], BF16)
    ones1r = sb("ones1r", [1, 128], BF16)
    ones1rf = sb("ones1rf", [1, 128], F32)
    onescol = sb("onescol", [128, 1], BF16)
    cpool = sb("cpool", [128, 8], F32)
    convw = sb("convw", [128, 128], F32)
    onorm = sb("onorm", [128, 2], F32)
    abc = sb("abc", [32, 2], F32)
    negexpA = sb("negexpA", [32, 1], F32)
    thr = sb("thr", [128, 8], F32)
    accbuf = sb("accbuf", [128, 4], F32)
    tcell = sb("tcell", [1, 2], F32)

    stb = sb("stb", [128, 8, VD], BF16)

    pbank = []
    for i, (nm, shp, dt) in enumerate([
            ("psA", [128, 512], F32), ("psB", [128, 512], F32),
            ("psC", [128, 512], F32), ("psD", [128, 512], F32),
            ("psT", [128, 1024], BF16), ("psU", [128, 512], F32),
            ("psO", [128, 512], F32), ("psW", [128, 512], F32)]):
        pbank.append(st.enter_context(nc.psum_tensor(nm, shp, dt)))
    psA, psB, psC, psD, psT, psU, psO, psW = pbank
    RB = ["pb0", "pb1", "pb2", "pb3", "pb4", "pb5", "pb6", "pb7"]

    p1 = contextlib.ExitStack()

    def sb1(name, shape, dt):
        return p1.enter_context(nc.sbuf_tensor(name, shape, dt))

    wq_s = sb1("wq_s", [128, 8, D], BF16)
    wk_s = sb1("wk_s", [128, 8, D], BF16)
    wv_s = sb1("wv_s", [128, 8, 2048], BF16)
    wg_s = sb1("wg_s", [128, 8, 2048], BF16)
    wa_s = sb1("wa_s", [128, 8, 32], BF16)
    wb_s = sb1("wb_s", [128, 8, 32], BF16)
    wo_s = sb1("wo_s", [128, 16, D], BF16)
    xin = sb1("xio", [128, D], F32)
    xres = xin
    xnorm = sb1("xnorm", [128, 512], BF16)
    xnt = sb1("xnt", [128, 8, BLK], BF16)
    qs = sb1("qs", [128, 8, BLK], BF16)
    ks = sb1("ks", [128, 8, BLK], BF16)
    vs = sb1("vs", [128, 16, BLK], BF16)
    sgate = sb1("sgate", [128, 16, 128], BF16)
    zbuf = sb1("zbuf", [128, 2, BLK + 8], BF16)
    cva = sb1("cva", [128, 2, BLK], BF16)
    cvb = sb1("cvb", [128, 2, BLK], BF16)
    halo = sb1("halo", [128, 32, 4], BF16)
    zarow = sb1("zarow", [32, BLK], F32)
    spw0 = sb1("spw0", [32, BLK], F32)
    grow = sb1("grow", [32, BLK], F32)
    lnbrow = sb1("lnbrow", [32, BLK], F32)
    lia = sb1("lia", [32, BLK], F32)
    lik = sb1("lik", [32, BLK], F32)
    rowstack = sb1("rowstack", [32, 1, 512], F32)
    colstack = sb1("colstack", [128, 1, 96], F32)
    nrb = sb1("nrb", [128, 1, 8], F32)
    gbh = sb1("gbh", [128, 1, 8], F32)
    bcols = sb1("bcols", [128, 1, 8], F32)
    NG = 2
    zpick = sb1("zpick", [32, NG, 384], F32)
    zpickc = sb1("zpickc", [32, NG, 1], F32)
    ktb = sb1("ktb", [128, NG, 128], BF16)
    y0b = sb1("y0b", [128, NG, 128], BF16)
    y1b = sb1("y1b", [128, NG, VD], BF16)
    dkb = sb1("dkb", [128, NG, 128], BF16)
    nkb = sb1("nkb", [128, NG, 128], BF16)
    nnb = sb1("nnb", [128, NG, 128], BF16)
    nnpb = sb1("nnpb", [128, NG, 128], BF16)
    mtb = sb1("mtb", [128, NG, 128], BF16)
    ebufb = sb1("ebufb", [128, NG, 384], F32)
    rqbuf = sb1("rqbuf", [128, NG, 4 * 128], BF16)
    pcb = sb1("pcb", [128, NG, 2, 128], BF16)
    x1b = sb1("x1b", [128, NG, 128], BF16)
    xbdb = sb1("xbdb", [128, NG, 128], BF16)
    mlitb = sb1("mlitb", [128, NG, 128], BF16)
    astb = sb1("astb", [128, NG, 128], BF16)
    wutb = sb1("wutb", [128, NG, 128], BF16)
    ubb = sb1("ubb", [128, NG, VD], BF16)
    gbcb = sb1("gbcb", [128, NG, 128], BF16)
    rqb = sb1("rqb", [128, NG, 128], BF16)
    rrow = sb1("rrow", [1, NG, 128], F32)
    rrowb = sb1("rrowb", [1, NG, 128], BF16)
    l127c = sb1("l127c", [128, NG, 1], F32)
    dgc = sb1("dgc", [128, NG, 1], F32)
    gamc = sb1("gamc", [128, NG, 1], F32)
    o_all = sb1("o_all", [128, 1, 16, 128], BF16)
    aw = sb1("aw", [128, D], F32)
    lnst = sb1("lnst", [128, 12], F32)
    mvst = sb1("mvst", [128, 2], F32)
    wcols = sb1("wcols", [128, 8], F32)
    wstage = sb1("wstage", [128, 256], F32)
    wabs = sb1("wabs", [128, 256], BF16)
    wtern = sb1("wtern", [128, 256], BF16)

    def T(fn, r, w):
        em.emit("tensor", fn, r, w)

    def V(fn, r, w, selfsync=False):
        em.emit("vector", fn, r, w, selfsync=selfsync)

    def S(fn, r, w, selfsync=True):
        em.emit("scalar", fn, r, w, selfsync=selfsync)

    def G(fn, r, w):
        em.emit("gpsimd", fn, r, w)

    def SD(fn, r, w):
        em.emit("sync", fn, r, w, dma=True)

    def GD(fn, r, w):
        em.emit("gpsimd", fn, r, w, dma=True)

    def pe_fence(bank_ap, bank_res, off):
        T(lambda e, p=bank_ap, o2=off: e.matmul(p[0:1, o2:o2 + 1], onescol[:, 0:1], onescol[:, 0:1],
                                                start=True, stop=True), ["onescol"], [bank_res])

    SD(lambda e: e.dma_start(out=ident_f[:, :], in_=CIDENT[0:32, 0:32]), [], ["ident_f"])
    GD(lambda e: e.dma_start(out=ident_b[:, :], in_=CIDENT.ap()), [], ["ident_b"])
    GD(lambda e: e.dma_start(out=mask3[:, :], in_=CMASK3.ap()), [], ["mask3"])
    GD(lambda e: e.dma_start(out=mbd[:, :], in_=CMBD.ap()), [], ["mbd"])
    GD(lambda e: e.dma_start(out=mbdT[:, :], in_=CMBD.ap()), [], ["mbdT"])
    GD(lambda e: e.dma_start(out=msbu[:, :], in_=CMSBU.ap()), [], ["msbu"])
    SD(lambda e: e.dma_start(out=ones32[:, :], in_=CONES32.ap()), [], ["ones32"])
    SD(lambda e: e.dma_start(out=epick[:, :], in_=CEPICK.ap()), [], ["epick"])
    GD(lambda e: e.dma_start(out=selq[:, :], in_=CSELQ.ap()), [], ["selq"])
    GD(lambda e: e.dma_start(out=ones1r[:, :], in_=CONES1R.ap()), [], ["ones1r"])
    SD(lambda e: e.dma_start(out=ones1rf[:, :], in_=CONES1R.ap()), [], ["ones1rf"])
    GD(lambda e: e.dma_start(out=onescol[:, :], in_=CONESCOL.ap()), [], ["onescol"])
    SD(lambda e: e.dma_start(out=cpool[:, :], in_=CCPOOL.ap()), [], ["cpool"])
    SD(lambda e: e.dma_start(out=convw[:, :], in_=CCONVW.ap()), [], ["convw"])
    SD(lambda e: e.dma_start(out=onorm[:, :], in_=CONORM.ap()), [], ["onorm"])
    SD(lambda e: e.dma_start(out=abc[:, :], in_=CABC.ap()), [], ["abc"])
    for dt_ in range(8):
        GD(lambda e, d=dt_: e.dma_start(out=wa_s[:, d, :], in_=WA[d * 128:(d + 1) * 128, :]), [], ["wa_s"])
        GD(lambda e, d=dt_: e.dma_start(out=wb_s[:, d, :], in_=WB[d * 128:(d + 1) * 128, :]), [], ["wb_s"])
    for dt_ in range(8):
        GD(lambda e, d=dt_: e.dma_start(out=wq_s[:, d, :], in_=WQ[d * 128:(d + 1) * 128, :]), [], ["wq_s"])
        GD(lambda e, d=dt_: e.dma_start(out=wk_s[:, d, :], in_=WK[d * 128:(d + 1) * 128, :]), [], ["wk_s"])
        GD(lambda e, d=dt_: e.dma_start(out=wv_s[:, d, :], in_=WV[d * 128:(d + 1) * 128, :]), [], ["wv_s"])
        GD(lambda e, d=dt_: e.dma_start(out=wg_s[:, d, :], in_=WG[d * 128:(d + 1) * 128, :]), [], ["wg_s"])
    for vt in range(16):
        GD(lambda e, v=vt: e.dma_start(out=wo_s[:, v, :], in_=WO[v * 128:(v + 1) * 128, :]), [], ["wo_s"])

    S(lambda e: e.activation(negexpA[:, 0:1], abc[:, 0:1], AF.Exp), ["abc"], ["negexpA"])
    V(lambda e: e.tensor_scalar(out=negexpA[:, 0:1], in0=negexpA[:, 0:1], scalar1=-1.0,
                                scalar2=None, op0=ALU.mult), ["negexpA"], ["negexpA"])
    V(lambda e: e.memset(accbuf[:, :], 0.0), [], ["accbuf"])
    V(lambda e: e.memset(lia[:, :], 0.0), [], ["lia"])
    V(lambda e: e.memset(lik[:, :], 0.0), [], ["lik"])

    quant_steps = []

    def q_mean_slab(W_, dti, cs, half):
        def go():
            SD(lambda e: e.dma_start(out=wstage[:, :], in_=W_[dti * 128:(dti + 1) * 128, cs:cs + 256]),
               ["wstage"], ["wstage"])
            S(lambda e: e.activation(wabs[:, :], wstage[:, :], AF.Abs,
                                     accum_out=wcols[:, 3:4]), ["wstage"], ["wabs", "wcols"])
            G(lambda e, hf=half: e.tensor_tensor(accbuf[:, hf:hf + 1], accbuf[:, hf:hf + 1],
                                                 wcols[:, 3:4], ALU.add), ["accbuf", "wcols"], ["accbuf"])
        return go

    def q_thresh(half, nelem):
        def go():
            T(lambda e, hf=half: e.matmul(psW[0:1, 0:1], cpool[:, 0:1], accbuf[:, hf:hf + 1],
                                          start=True, stop=True), ["accbuf", "cpool"], [RB[7]])
            pe_fence(psW, RB[7], 500)
            V(lambda e: e.tensor_scalar(out=tcell[0:1, 0:1], in0=psW[0:1, 0:1],
                                        scalar1=1.0 / nelem, scalar2=1e-5, op0=ALU.mult, op1=ALU.max),
              [RB[7]], ["tcell"])
            T(lambda e: e.matmul(psW[:, 0:1], ones1rf[:, :], tcell[0:1, 0:1],
                                 start=True, stop=True), ["tcell", "ones1rf"], [RB[7]])
            base = half * 3
            pe_fence(psW, RB[7], 501)
            V(lambda e, b3=base: e.tensor_scalar(out=thr[:, b3:b3 + 1], in0=psW[:, 0:1], scalar1=0.5,
                                                 scalar2=None, op0=ALU.mult), [RB[7]], ["thr"])
            V(lambda e, b3=base: e.tensor_scalar(out=thr[:, b3 + 1:b3 + 2], in0=psW[:, 0:1], scalar1=-0.5,
                                                 scalar2=None, op0=ALU.mult), [RB[7]], ["thr"])
            V(lambda e, b3=base: e.tensor_copy(thr[:, b3 + 2:b3 + 3], psW[:, 0:1]), [RB[7]], ["thr"])
        return go

    def q_tern_slab(W_, TOUT, dti, cs, half, scale_mean):
        def go():
            SD(lambda e: e.dma_start(out=wstage[:, :], in_=W_[dti * 128:(dti + 1) * 128, cs:cs + 256]),
               ["wstage"], ["wstage"])
            base = half * 3
            G(lambda e, b3=base: e.tensor_scalar(out=wabs[:, :], in0=wstage[:, :], scalar1=thr[:, b3:b3 + 1],
                                                 scalar2=None, op0=ALU.is_ge), ["wstage", "thr"], ["wabs"])
            G(lambda e, b3=base: e.tensor_scalar(out=wtern[:, :], in0=wstage[:, :], scalar1=thr[:, b3 + 1:b3 + 2],
                                                 scalar2=None, op0=ALU.is_le), ["wstage", "thr"], ["wtern"])
            if scale_mean:
                G(lambda e: e.tensor_tensor(wabs[:, :], wabs[:, :], wtern[:, :], ALU.subtract),
                  ["wabs", "wtern"], ["wabs"])
                G(lambda e, b3=base: e.tensor_scalar(out=wtern[:, :], in0=wabs[:, :], scalar1=thr[:, b3 + 2:b3 + 3],
                                                     scalar2=None, op0=ALU.mult), ["wabs", "thr"], ["wtern"])
            else:
                G(lambda e: e.tensor_tensor(wtern[:, :], wabs[:, :], wtern[:, :], ALU.subtract),
                  ["wabs", "wtern"], ["wtern"])
            SD(lambda e: e.dma_start(out=TOUT[dti * 128:(dti + 1) * 128, cs:cs + 256], in_=wtern[:, :]),
               ["wtern"], ["tern_d"])
        return go

    if do_pass2:
        for dti in range(8):
            for cs in range(0, HID, 256):
                quant_steps.append(q_mean_slab(W1T, dti, cs, 0))
        for dti in range(32):
            for cs in range(0, D, 256):
                quant_steps.append(q_mean_slab(W2T, dti, cs, 1))
        quant_steps.append(q_thresh(0, D * HID))
        quant_steps.append(q_thresh(1, D * HID))
        for dti in range(8):
            for cs in range(0, HID, 256):
                quant_steps.append(q_tern_slab(W1T, TERN1, dti, cs, 0, True))
        for dti in range(32):
            for cs in range(0, D, 256):
                quant_steps.append(q_tern_slab(W2T, TERN2, dti, cs, 1, False))
    qstep_i = [0]

    def run_quant_steps(n):
        for _ in range(n):
            if qstep_i[0] < len(quant_steps):
                quant_steps[qstep_i[0]]()
                qstep_i[0] += 1

    def pass1_block(b, blk):
        t0 = blk * BLK
        if blk == 0:
            G(lambda e: e.memset(halo[:, :, :], 0.0), ["halo"], ["halo"])
        for tt in range(2):
            trow = t0 + tt * 128
            SD(lambda e, b=b, tr=trow: e.dma_start(out=xin[:, :], in_=X[b, tr:tr + 128, :]),
               ["xio"], ["xio"])
            V(lambda e: e.bn_stats(lnst[:, 0:6], xin[:, 0:512]), ["xio"], ["lnst"])
            V(lambda e: e.bn_stats(lnst[:, 6:12], xin[:, 512:1024]), ["xio"], ["lnst"])
            V(lambda e: e.bn_aggr(mvst[:, 0:2], lnst[:, 0:12]), ["lnst"], ["mvst"], selfsync=True)
            S(lambda e: e.activation(wcols[:, 0:1], mvst[:, 1:2], AF.Ln, bias=cpool[:, 1:2]),
              ["mvst", "cpool"], ["wcols"])
            S(lambda e: e.activation(wcols[:, 0:1], wcols[:, 0:1], AF.Exp, scale=-0.5),
              ["wcols"], ["wcols"])
            for hf in range(2):
                V(lambda e, h3=hf: e.tensor_scalar(out=xnorm[:, :], in0=xin[:, h3 * 512:(h3 + 1) * 512],
                                                   scalar1=mvst[:, 0:1], scalar2=wcols[:, 0:1],
                                                   op0=ALU.subtract, op1=ALU.mult),
                  ["xio", "mvst", "wcols"], ["xnorm"])
                for dt_ in range(4):
                    T(lambda e, d=dt_: e.transpose(psT[:, d * 128:(d + 1) * 128],
                                                   xnorm[:, d * 128:(d + 1) * 128], ident_b[:, :]),
                      ["xnorm", "ident_b"], [RB[4]])
                for dt_ in range(4):
                    S(lambda e, d=dt_, h3=hf, ttt=tt: e.copy(xnt[:, h3 * 4 + d, ttt * 128:(ttt + 1) * 128],
                                                             psT[:, d * 128:(d + 1) * 128]),
                      [RB[4]], ["xnt"])

        plan = []
        for i in range(8):
            plan.append((wq_s, i, qs, i, 0 + 4 * i, "conv"))
        for i in range(8):
            plan.append((wk_s, i, ks, i, 32 + 4 * i, "conv"))
        for i in range(16):
            plan.append((wv_s, i, vs, i, 64 + 4 * i, "conv"))

        silu_jobs = []
        for pi, (wsb, ci, dst, di, cvbase, kind) in enumerate(plan):
            bank = pi % 2
            ps = psC if bank == 0 else psD
            pr = RB[2 + bank]
            wname = {id(wq_s): "wq_s", id(wk_s): "wk_s", id(wv_s): "wv_s"}[id(wsb)]
            for dt_ in range(8):
                T(lambda e, d=dt_, w=wsb, c=ci, p=ps: e.matmul(
                    p[:, 0:BLK], w[:, d, c * 128:(c + 1) * 128], xnt[:, d, :],
                    start=(d == 0), stop=(d == 7)), [wname, "xnt"], [pr])
            if True:
                par = pi % 2
                zb_ = zbuf
                S(lambda e, p=ps, pa=par: e.copy(zb_[:, pa, 4:4 + BLK], p[:, 0:BLK]),
                  [pr], ["zbuf%d" % par])
                V(lambda e, pa=par, cb=cvbase // 4: e.tensor_copy(zb_[:, pa, 1:4], halo[:, cb, 0:3]),
                  ["halo", "zbuf%d" % par], ["zbuf%d" % par])
                V(lambda e, pa=par, cb=cvbase // 4: e.tensor_copy(halo[:, cb, 0:3], zb_[:, pa, 1 + BLK:4 + BLK]),
                  ["zbuf%d" % par], ["halo"])
                for j in range(4):
                    dstc = cva if j == 0 else cvb
                    dn = ("cva%d" % par) if j == 0 else ("cvb%d" % par)
                    S(lambda e, pa=par, cv=cvbase, jj=j, dc=dstc: e.activation(
                        dc[:, pa, :], zb_[:, pa, 1 + jj:1 + jj + BLK], AF.Copy,
                        scale=convw[:, cv + jj:cv + jj + 1]), ["zbuf%d" % par, "convw"], [dn])
                    if j > 0:
                        V(lambda e, pa=par: e.tensor_tensor(cva[:, pa, :], cva[:, pa, :], cvb[:, pa, :],
                                                            ALU.add), ["cva%d" % par, "cvb%d" % par], ["cva%d" % par])
                silu_jobs.append((par, dst, di, {id(qs): "qs", id(ks): "ks", id(vs): "vs"}[id(dst)], pi))
                if len(silu_jobs) == 2 or pi == len(plan) - 1:
                    for (pa2, dd2, di2, dn2, _) in silu_jobs:
                        S(lambda e, pa3=pa2, dd3=dd2, di3=di2: e.activation(dd3[:, di3, :], cva[:, pa3, :], AF.Silu),
                          ["cva%d" % pa2], [dn2])
                    silu_jobs = []

        for dt_ in range(8):
            T(lambda e, d=dt_: e.matmul(psC[0:32, 0:BLK], wa_s[:, d, :], xnt[:, d, :],
                                        start=(d == 0), stop=(d == 7)), ["wa_s", "xnt"], [RB[2]])
        S(lambda e: e.copy(zarow[:, :], psC[0:32, 0:BLK]), [RB[2]], ["zarow"])
        for dt_ in range(8):
            T(lambda e, d=dt_: e.matmul(psD[0:32, 0:BLK], wb_s[:, d, :], xnt[:, d, :],
                                        start=(d == 0), stop=(d == 7)), ["wb_s", "xnt"], [RB[3]])

        for (src, srcn, dstrow, dstn, bank) in ((qs, "qs", lia, "lia", 0), (ks, "ks", lik, "lik", 1)):
            ps = psA if bank == 0 else psB
            pr = RB[bank]
            for hh in range(8):
                S(lambda e, h2=hh, s2=src: e.activation(cvb[:, 0, :], s2[:, h2, :], AF.Square),
                  [srcn, "cvb0"], ["cvb0"])
                T(lambda e, h2=hh: e.matmul(ps[0:8, BLK:2 * BLK], selq[:, h2 * 8:(h2 + 1) * 8], cvb[:, 0, :],
                                            start=(h2 == 0), stop=(h2 == 7)), ["cvb0", "selq"], [pr])
            V(lambda e: e.tensor_scalar(out=spw0[0:8, :], in0=ps[0:8, BLK:2 * BLK], scalar1=1e-12,
                                        scalar2=None, op0=ALU.max), [pr], ["spw0"])
            S(lambda e: e.activation(spw0[0:8, :], spw0[0:8, :], AF.Ln), ["spw0"], ["spw0"])
            V(lambda e, dr=dstrow: e.tensor_scalar(out=dr[0:8, :], in0=spw0[0:8, :], scalar1=-0.5,
                                                   scalar2=None, op0=ALU.mult), ["spw0"], [dstn])

        V(lambda e: e.tensor_scalar(out=zarow[:, :], in0=zarow[:, :], scalar1=abc[:, 1:2],
                                    scalar2=None, op0=ALU.add), ["zarow", "abc"], ["zarow"])
        S(lambda e: e.activation(spw0[:, :], zarow[:, :], AF.Relu), ["zarow"], ["spw0"])
        S(lambda e: e.activation(grow[:, :], zarow[:, :], AF.Abs), ["zarow"], ["grow"])
        S(lambda e: e.activation(grow[:, :], grow[:, :], AF.Exp, scale=-1.0), ["grow"], ["grow"])
        S(lambda e: e.activation(grow[:, :], grow[:, :], AF.Ln, bias=cpool[0:32, 0:1]), ["grow", "cpool"], ["grow"])
        V(lambda e: e.tensor_tensor(spw0[:, :], spw0[:, :], grow[:, :], ALU.add), ["spw0", "grow"], ["spw0"])
        V(lambda e: e.tensor_scalar(out=grow[:, :], in0=spw0[:, :], scalar1=negexpA[:, 0:1],
                                    scalar2=None, op0=ALU.mult), ["spw0", "negexpA"], ["grow"])
        S(lambda e: e.copy(zarow[:, :], psD[0:32, 0:BLK]), [RB[3], "zarow"], ["zarow"])
        S(lambda e: e.activation(spw0[:, :], zarow[:, :], AF.Relu, scale=-1.0), ["zarow", "spw0"], ["spw0"])
        S(lambda e: e.activation(lnbrow[:, :], zarow[:, :], AF.Abs), ["zarow"], ["lnbrow"])
        S(lambda e: e.activation(lnbrow[:, :], lnbrow[:, :], AF.Exp, scale=-1.0), ["lnbrow"], ["lnbrow"])
        S(lambda e: e.activation(lnbrow[:, :], lnbrow[:, :], AF.Ln, bias=cpool[0:32, 0:1]), ["lnbrow", "cpool"], ["lnbrow"])
        V(lambda e: e.tensor_tensor(spw0[:, :], spw0[:, :], lnbrow[:, :], ALU.add), ["spw0", "lnbrow"], ["spw0"])
        V(lambda e: e.tensor_scalar(out=lnbrow[:, :], in0=spw0[:, :], scalar1=np.log(2.0), scalar2=-1.0,
                                    op0=ALU.subtract, op1=ALU.mult), ["spw0"], ["lnbrow"])

        if cfg.get("dump") and b == 0 and blk == 0:
            SD(lambda e: e.dma_start(out=DBG["qs"].ap(), in_=qs[:, :, :].rearrange("p a b -> p (a b)")), ["qs"], ["dbgd"])
            SD(lambda e: e.dma_start(out=DBG["ks"].ap(), in_=ks[:, :, :].rearrange("p a b -> p (a b)")), ["ks"], ["dbgd"])
            SD(lambda e: e.dma_start(out=DBG["vs"].ap(), in_=vs[:, :, :].rearrange("p a b -> p (a b)")), ["vs"], ["dbgd"])
            SD(lambda e: e.dma_start(out=DBG["grow"].ap(), in_=grow[:, :]), ["grow"], ["dbgd"])
            SD(lambda e: e.dma_start(out=DBG["lnbrow"].ap(), in_=lnbrow[:, :]), ["lnbrow"], ["dbgd"])
            SD(lambda e: e.dma_start(out=DBG["lia"].ap(), in_=lia[:, :]), ["lia"], ["dbgd"])
            SD(lambda e: e.dma_start(out=DBG["lik"].ap(), in_=lik[:, :]), ["lik"], ["dbgd"])
            SD(lambda e: e.dma_start(out=DBG["xnt"].ap(), in_=xnt[:, :, :].rearrange("p a b -> p (a b)")), ["xnt"], ["dbgd"])

        for c in range(2):
            cp = 0
            cs = c * 128
            rs = "rows%d" % cp
            V(lambda e, pa=cp, c0=cs: e.tensor_tensor_scan(
                rowstack[:, pa, 0:128], grow[:, c0:c0 + 128], grow[:, c0:c0 + 128], 0.0,
                ALU.add, ALU.bypass), ["grow", rs], [rs])
            V(lambda e, pa=cp, c0=cs: e.tensor_tensor(rowstack[:, pa, 128:256], rowstack[:, pa, 0:128],
                                                      lnbrow[:, c0:c0 + 128], ALU.add), [rs, "lnbrow"], [rs])
            V(lambda e, pa=cp, c0=cs: e.tensor_tensor(rowstack[:, pa, 128:256], rowstack[:, pa, 128:256],
                                                      lik[:, c0:c0 + 128], ALU.add), [rs, "lik"], [rs])
            V(lambda e, pa=cp, c0=cs: e.tensor_tensor(rowstack[:, pa, 256:384], rowstack[:, pa, 0:128],
                                                      lik[:, c0:c0 + 128], ALU.subtract), [rs, "lik"], [rs])
            V(lambda e, pa=cp, c0=cs: e.tensor_tensor(rowstack[:, pa, 384:512], rowstack[:, pa, 0:128],
                                                      lia[:, c0:c0 + 128], ALU.add), [rs, "lia"], [rs])

            for ri, off in ((1, 0), (2, 32)):
                T(lambda e, pa=cp, r2=ri, o2=off: e.transpose(
                    psW[:, o2:o2 + 32], rowstack[:, pa, r2 * 128:(r2 + 1) * 128],
                    ident_f[:, :]), [rs, "ident_f"], [RB[7]])
            T(lambda e, c0=cs: e.transpose(psW[:, 64:96], lnbrow[:, c0:c0 + 128],
                                           ident_f[:, :]), ["lnbrow", "ident_f"], [RB[7]])
            cst = "cols%d" % cp
            V(lambda e, pa=cp: e.tensor_copy(colstack[:, pa, 0:96], psW[:, 0:96]), [RB[7]], [cst])
            V(lambda e, pa=cp: e.tensor_scalar(out=nrb[:, pa, 0:8], in0=colstack[:, pa, 32:40],
                                               scalar1=-1.0, scalar2=None, op0=ALU.mult), [cst], ["nrb%d" % cp])
            S(lambda e, pa=cp: e.activation(gbh[:, pa, 0:8], colstack[:, pa, 0:8], AF.Exp), [cst], ["gbh%d" % cp])
            S(lambda e, pa=cp: e.activation(bcols[:, pa, 0:8], colstack[:, pa, 64:72], AF.Exp), [cst], ["bcol%d" % cp])

            for gi in range(16):
                bank = gi % 2
                psg = psC if bank == 0 else psD
                prg = RB[2 + bank]
                for dt_ in range(8):
                    T(lambda e, d=dt_, g3=gi, p=psg, c0=cs: e.matmul(
                        p[:, 0:128], wg_s[:, d, g3 * 128:(g3 + 1) * 128], xnt[:, d, c0:c0 + 128],
                        start=(d == 0), stop=(d == 7)), ["wg_s", "xnt"], [prg])
                S(lambda e, g3=gi, p=psg: e.activation(sgate[:, g3, :], p[:, 0:128], AF.Silu),
                  [prg], ["sgate"])
                V(lambda e, g3=gi: e.tensor_scalar(
                    out=sgate[:, g3, :], in0=sgate[:, g3, :], scalar1=onorm[:, g3 % 2:g3 % 2 + 1],
                    scalar2=None, op0=ALU.mult), ["sgate", "onorm"], ["sgate"])

            first_chunk = (blk == 0 and c == 0)
            ygates = cfg.get("ygates")

            def yg(i):
                return ygates is None or i in ygates

            def gen_group(hh):
                gp = hh % NG
                gn = "g%d" % gp
                bE = psA if gp == 0 else psU
                bO = psB if gp == 0 else psO
                bEr = RB[0] if gp == 0 else RB[5]
                bOr = RB[1] if gp == 0 else RB[6]
                tb = 512 * gp
                slot = hh
                V(lambda e, pa=cp, h2=hh, g2=gp: e.tensor_scalar(
                    out=zpick[:, g2, :], in0=rowstack[:, pa, 128:512], scalar1=epick[:, h2:h2 + 1],
                    scalar2=None, op0=ALU.mult), [rs, "epick"], ["zp" + gn])
                V(lambda e, pa=cp, h2=hh, g2=gp: e.tensor_scalar(
                    out=zpickc[:, g2, :], in0=rowstack[:, pa, 127:128], scalar1=epick[:, h2:h2 + 1],
                    scalar2=None, op0=ALU.mult), [rs, "epick"], ["zc" + gn])
                if cfg.get("nobigbc"):
                    for j3 in range(3):
                        T(lambda e, g2=gp, j2=j3: e.matmul(
                            bE[:, j2 * 128:(j2 + 1) * 128], ones32[:, :], zpick[:, g2, j2 * 128:(j2 + 1) * 128],
                            start=True, stop=True), ["zp" + gn, "ones32"], [bEr])
                else:
                    T(lambda e, g2=gp: e.matmul(bE[:, 0:384], ones32[:, :], zpick[:, g2, :],
                                                start=True, stop=True), ["zp" + gn, "ones32"], [bEr])
                T(lambda e, g2=gp: e.matmul(bO[:, 0:1], ones32[:, :], zpickc[:, g2, :],
                                            start=True, stop=True), ["zc" + gn, "ones32"], [bOr])
                V(lambda e, g2=gp, h2=hh, c0=cs: e.tensor_scalar(
                    out=nkb[:, g2, :], in0=ks[:, h2, c0:c0 + 128], scalar1=-1.0, scalar2=None,
                    op0=ALU.mult), ["ks"], ["nk" + gn])
                T(lambda e, g2=gp, h2=hh, c0=cs: e.matmul(bE[:, 384:512], nkb[:, g2, :],
                                                          ks[:, h2, c0:c0 + 128], start=True, stop=True),
                  ["nk" + gn, "ks"], [bEr])
                T(lambda e, g2=gp, h2=hh, c0=cs: e.matmul(bO[:, 1:129], ks[:, h2, c0:c0 + 128],
                                                          qs[:, h2, c0:c0 + 128], start=True, stop=True),
                  ["ks", "qs"], [bOr])
                pe_fence(bO, bOr, 135)
                V(lambda e, g2=gp: e.tensor_copy(l127c[:, g2, :], bO[:, 0:1]), [bOr], ["l127" + gn])
                if yg(1):
                    yield
                if not cfg.get("batchmm"):
                    prodspec = [
                        (0, "nnp" + gn, nnpb, bEr, bE, 384, False),
                        (128, "nn" + gn, nnb, bEr, bE, 384, True),
                        (256, "mt" + gn, mtb, bOr, bO, 1, False),
                    ]
                    for (off3, pn3, pb3, pres3, pbank3, poff3, neg3) in prodspec:
                        V(lambda e, g2=gp, o3=off3: e.tensor_tensor(
                            ebufb[:, g2, o3:o3 + 128], bE[:, o3:o3 + 128], mask3[:, o3:o3 + 128],
                            ALU.add), [bEr, "mask3"], ["eb" + gn])
                        if neg3:
                            S(lambda e, g2=gp, pa=cp, h2=hh, o3=off3: e.activation(
                                ebufb[:, g2, o3:o3 + 128], ebufb[:, g2, o3:o3 + 128], AF.Exp,
                                scale=-1.0, bias=colstack[:, pa, h2:h2 + 1]),
                              ["eb" + gn, "cols%d" % cp], ["eb" + gn])
                        else:
                            S(lambda e, g2=gp, pa=cp, h2=hh, o3=off3: e.activation(
                                ebufb[:, g2, o3:o3 + 128], ebufb[:, g2, o3:o3 + 128], AF.Exp,
                                bias=nrb[:, pa, h2:h2 + 1]),
                              ["eb" + gn, "nrb%d" % cp], ["eb" + gn])
                        V(lambda e, g2=gp, o3=off3, b3=pb3, p4=pbank3, po3=poff3: e.tensor_tensor(
                            b3[:, g2, :], ebufb[:, g2, o3:o3 + 128], p4[:, po3:po3 + 128], ALU.mult),
                          ["eb" + gn, pres3], [pn3])
                        if yg(2):
                            yield
                else:
                    V(lambda e, g2=gp: e.tensor_tensor(ebufb[:, g2, :], bE[:, 0:384], mask3[:, :], ALU.add),
                      [bEr, "mask3"], ["eb" + gn])
                    S(lambda e, g2=gp, pa=cp, h2=hh: e.activation(ebufb[:, g2, 0:128], ebufb[:, g2, 0:128],
                                                                  AF.Exp, bias=nrb[:, pa, h2:h2 + 1]),
                      ["eb" + gn, "nrb%d" % cp], ["eb" + gn])
                    S(lambda e, g2=gp, pa=cp, h2=hh: e.activation(ebufb[:, g2, 128:256], ebufb[:, g2, 128:256],
                                                                  AF.Exp, scale=-1.0, bias=colstack[:, pa, h2:h2 + 1]),
                      ["eb" + gn, "cols%d" % cp], ["eb" + gn])
                    S(lambda e, g2=gp, pa=cp, h2=hh: e.activation(ebufb[:, g2, 256:384], ebufb[:, g2, 256:384],
                                                                  AF.Exp, bias=nrb[:, pa, h2:h2 + 1]),
                      ["eb" + gn, "nrb%d" % cp], ["eb" + gn])
                    if yg(2):
                        yield
                    V(lambda e, g2=gp: e.tensor_tensor(nnpb[:, g2, :], ebufb[:, g2, 0:128], bE[:, 384:512],
                                                       ALU.mult), ["eb" + gn, bEr], ["nnp" + gn])
                    V(lambda e, g2=gp: e.tensor_tensor(nnb[:, g2, :], ebufb[:, g2, 128:256], bE[:, 384:512],
                                                       ALU.mult), ["eb" + gn, bEr], ["nn" + gn])
                    V(lambda e, g2=gp: e.tensor_tensor(mtb[:, g2, :], ebufb[:, g2, 256:384], bO[:, 1:129],
                                                       ALU.mult), ["eb" + gn, bOr], ["mt" + gn])
                if yg(4):
                    yield
                S(lambda e, g2=gp: e.activation(gbcb[:, g2, :], bE[:, 256:384], AF.Exp), [bEr], ["gbc" + gn])
                V(lambda e, g2=gp, h2=hh, c0=cs: e.tensor_tensor(rqb[:, g2, :], gbcb[:, g2, :],
                                                                 qs[:, h2, c0:c0 + 128], ALU.mult),
                  ["gbc" + gn, "qs"], ["rq" + gn])
                T(lambda e, h2=hh, c0=cs, t2=tb: e.transpose(psT[:, t2:t2 + 128], ks[:, h2, c0:c0 + 128],
                                                             ident_b[:, :]), ["ks", "ident_b"], [RB[4]])
                V(lambda e, g2=gp, t2=tb: e.tensor_copy(ktb[:, g2, :], psT[:, t2:t2 + 128]), [RB[4]], ["kt" + gn])
                V(lambda e, g2=gp, pa=cp, h2=hh, t2=tb: e.tensor_scalar(
                    out=y0b[:, g2, :], in0=psT[:, t2:t2 + 128], scalar1=gbh[:, pa, h2:h2 + 1], scalar2=None,
                    op0=ALU.mult), [RB[4], "gbh%d" % cp], ["y0" + gn])
                if yg(5):
                    yield
                for vt2 in range(2):
                    T(lambda e, h2=hh, c0=cs, v2=vt2, t2=tb: e.transpose(
                        psT[:, t2 + 128 + v2 * 128:t2 + 256 + v2 * 128], vs[:, 2 * h2 + v2, c0:c0 + 128],
                        ident_b[:, :]), ["vs", "ident_b"], [RB[4]])
                V(lambda e, g2=gp, pa=cp, h2=hh, t2=tb: e.tensor_scalar(
                    out=y1b[:, g2, :], in0=psT[:, t2 + 128:t2 + 384], scalar1=bcols[:, pa, h2:h2 + 1],
                    scalar2=None, op0=ALU.mult), [RB[4], "bcol%d" % cp], ["y1" + gn])
                if yg(6):
                    yield
                db = [(bE, bEr, 0), (bO, bOr, 129)]
                mmi = [0]

                def dmm(lhs_name, lhs_ap, rhs_name, rhs_ap, out_name, out_ap, sub_from=None):
                    bk, bkr, off = db[mmi[0] % 2]
                    mmi[0] += 1
                    T(lambda e, l2=lhs_ap, r2=rhs_ap, p=bk, o2=off: e.matmul(
                        p[:, o2:o2 + 128], l2, r2, start=True, stop=True), [lhs_name, rhs_name], [bkr])
                    if sub_from is None:
                        V(lambda e, p=bk, o3=out_ap, o2=off: e.tensor_copy(o3, p[:, o2:o2 + 128]),
                          [bkr], [out_name])
                    else:
                        sf_name, sf_ap = sub_from
                        V(lambda e, p=bk, o3=out_ap, s2=sf_ap, o2=off: e.tensor_tensor(
                            o3, s2, p[:, o2:o2 + 128], ALU.subtract), [bkr, sf_name], [out_name])

                def padd(lhs_name, lhs_ap, rhs_name, rhs_ap, out_name, out_ap):
                    bk, bkr, off = db[mmi[0] % 2]
                    mmi[0] += 1
                    T(lambda e, l2=lhs_ap, r2=rhs_ap, p=bk, o2=off: e.matmul(
                        p[:, o2:o2 + 128], l2, r2, start=True, stop=True), [lhs_name, rhs_name], [bkr])
                    V(lambda e, p=bk, o3=out_ap, s2=rhs_ap, o2=off: e.tensor_tensor(
                        o3, s2, p[:, o2:o2 + 128], ALU.add), [bkr, rhs_name], [out_name])

                rq = "rq2" + gn

                def rqa(sl):
                    return rqbuf[:, gp, sl * 128:(sl + 1) * 128]
                V(lambda e, g2=gp: e.tensor_tensor(ebufb[:, g2, 0:128], nnpb[:, g2, :], mbd[:, :], ALU.mult),
                  ["nnp" + gn, "mbd"], ["eb" + gn])
                V(lambda e, g2=gp: e.tensor_copy(pcb[:, g2, 0, :], ebufb[:, g2, 0:128]), ["eb" + gn], ["pc" + gn])
                V(lambda e, g2=gp: e.tensor_tensor(nkb[:, g2, :], nnb[:, g2, :], mbdT[:, :], ALU.mult),
                  ["nn" + gn, "mbdT"], ["nk" + gn])
                V(lambda e, g2=gp: e.tensor_tensor(astb[:, g2, :], nnpb[:, g2, :], msbu[:, :], ALU.mult),
                  ["nnp" + gn, "msbu"], ["ast" + gn])
                if yg(7):
                    yield
                nbdp_n, nbdp = "pc" + gn, pcb[:, gp, 0, :]
                nbdn_n, nbdn = "nk" + gn, nkb[:, gp, :]
                dmm(nbdp_n, nbdp, nbdn_n, nbdn, rq, rqa(0))
                dmm(nbdn_n, nbdn, nbdp_n, nbdp, rq, rqa(1))
                V(lambda e, g2=gp: e.tensor_tensor(pcb[:, g2, 1, :], pcb[:, g2, 0, :], ident_b[:, :],
                                                   ALU.add), ["pc" + gn, "ident_b"], ["pc" + gn])
                if yg(8):
                    yield
                padd(rq, rqa(0), "pc" + gn, pcb[:, gp, 1, :], "pc" + gn, pcb[:, gp, 0, :])
                if yg(9):
                    yield
                dmm(rq, rqa(1), rq, rqa(0), rq, rqa(2))
                if yg(10):
                    yield
                dmm(rq, rqa(0), rq, rqa(1), rq, rqa(3))
                if yg(11):
                    yield
                padd(rq, rqa(2), "pc" + gn, pcb[:, gp, 0, :], "pc" + gn, pcb[:, gp, 1, :])
                if yg(12):
                    yield
                dmm(rq, rqa(3), rq, rqa(2), rq, rqa(0))
                if yg(13):
                    yield
                padd(rq, rqa(0), "pc" + gn, pcb[:, gp, 1, :], "xbd" + gn, xbdb[:, gp, :])
                if yg(14):
                    yield
                T(lambda e, g2=gp, t2=tb: e.transpose(psT[:, t2 + 384:t2 + 512], xbdb[:, g2, :], ident_b[:, :]),
                  ["xbd" + gn, "ident_b"], [RB[4]])
                V(lambda e, g2=gp, t2=tb: e.tensor_copy(nkb[:, g2, :], psT[:, t2 + 384:t2 + 512]),
                  [RB[4]], ["nk" + gn])
                if yg(15):
                    yield
                dmm("ast" + gn, astb[:, gp, :], "nk" + gn, nkb[:, gp, :], "mlit" + gn, mlitb[:, gp, :])
                if yg(16):
                    yield
                wnames = ["xbd" + gn, "pc" + gn, "pc" + gn]
                waps = [xbdb[:, gp, :], pcb[:, gp, 0, :], pcb[:, gp, 1, :]]
                cur = 0
                for it in range(7):
                    nxt2 = 1 + (it % 2)
                    last = (it == 6)
                    out_name = ("x1" + gn) if last else wnames[nxt2]
                    out_ap = x1b[:, gp, :] if last else waps[nxt2]
                    dmm("mlit" + gn, mlitb[:, gp, :], wnames[cur], waps[cur], out_name, out_ap,
                        sub_from=("xbd" + gn, xbdb[:, gp, :]))
                    cur = nxt2
                    if yg(17):
                        yield
                T(lambda e, g2=gp: e.matmul(bE[:, 256:384], y0b[:, g2, :], x1b[:, g2, :],
                                            start=True, stop=True), ["y0" + gn, "x1" + gn], [bEr])
                V(lambda e, g2=gp: e.tensor_scalar(out=wutb[:, g2, :], in0=bE[:, 256:384], scalar1=-1.0,
                                                   scalar2=None, op0=ALU.mult), [bEr], ["wut" + gn])
                if yg(18):
                    yield
                T(lambda e, g2=gp, fc=first_chunk: e.matmul(bE[:, 0:256], x1b[:, g2, :], y1b[:, g2, :],
                                                            start=True, stop=fc), ["x1" + gn, "y1" + gn], [bEr])
                if not first_chunk:
                    T(lambda e, g2=gp, sl2=slot: e.matmul(bE[:, 0:256], wutb[:, g2, :], stb[:, sl2, :],
                                                          start=False, stop=True), ["wut" + gn, "stb"], [bEr])
                V(lambda e, g2=gp: e.tensor_copy(ubb[:, g2, :], bE[:, 0:256]), [bEr], ["ub" + gn])
                if yg(19):
                    yield
                for vt2 in range(2):
                    T(lambda e, g2=gp, v2=vt2, fc=first_chunk: e.matmul(
                        bE[:, v2 * 128:(v2 + 1) * 128], ubb[:, g2, v2 * 128:(v2 + 1) * 128],
                        mtb[:, g2, :], start=True, stop=fc), ["ub" + gn, "mt" + gn], [bEr])
                    if not first_chunk:
                        T(lambda e, g2=gp, v2=vt2, sl2=slot: e.matmul(
                            bE[:, v2 * 128:(v2 + 1) * 128], stb[:, sl2, v2 * 128:(v2 + 1) * 128],
                            rqb[:, g2, :], start=False, stop=True), ["stb", "rq" + gn], [bEr])
                    if yg(20):
                        yield
                if cfg.get("dump") and b == 0 and blk == 0 and c == 0 and hh == cfg.get("dump_head", 0):
                    for nm2, buf2, res2 in [("nnp", nnpb[:, gp, :], "nnp" + gn), ("nn", nnb[:, gp, :], "nn" + gn),
                                            ("mt", mtb[:, gp, :], "mt" + gn), ("x1", x1b[:, gp, :], "x1" + gn),
                                            ("y0", y0b[:, gp, :], "y0" + gn), ("y1", y1b[:, gp, :], "y1" + gn),
                                            ("ub", ubb[:, gp, :], "ub" + gn), ("gbc", gbcb[:, gp, :], "gbc" + gn)]:
                        SD(lambda e, n3=nm2, b3=buf2: e.dma_start(out=DBG[n3].ap(), in_=b3),
                           [res2], ["dbgd"])
                S(lambda e, g2=gp: e.activation(rqbuf[:, g2, 0:256], bE[:, 0:256], AF.Square),
                  [bEr, rq], [rq])
                for vt2 in range(2):
                    T(lambda e, g2=gp, v2=vt2: e.matmul(bE[0:1, 256:384], onescol[:, :],
                                                        rqbuf[:, g2, v2 * 128:(v2 + 1) * 128],
                                                        start=(v2 == 0), stop=(v2 == 1)),
                      [rq, "onescol"], [bEr])
                if yg(21):
                    yield
                pe_fence(bE, bEr, 400)
                S(lambda e, g2=gp: e.activation(rrow[0:1, g2, :], bE[0:1, 256:384], AF.Ln, scale=1.0 / VD,
                                                bias=cpool[0:1, 1:2]), [bEr, "cpool"], ["rr" + gn])
                S(lambda e, g2=gp: e.activation(rrow[0:1, g2, :], rrow[0:1, g2, :], AF.Exp, scale=-0.5),
                  ["rr" + gn], ["rr" + gn])
                V(lambda e, g2=gp: e.tensor_copy(rrowb[0:1, g2, :], rrow[0:1, g2, :]), ["rr" + gn], ["rrb" + gn])
                T(lambda e, g2=gp: e.matmul(bE[:, 384:512], ones1r[:, :], rrowb[0:1, g2, :],
                                            start=True, stop=True), ["rrb" + gn, "ones1r"], [bEr])
                if yg(22):
                    yield
                for vt2 in range(2):
                    V(lambda e, g2=gp, h2=hh, v2=vt2: e.tensor_tensor(
                        o_all[:, 0, 2 * h2 + v2, :], bE[:, v2 * 128:(v2 + 1) * 128],
                        sgate[:, 2 * h2 + v2, 0:128], ALU.mult), [bEr, "sgate"], ["oall"])
                for vt2 in range(2):
                    V(lambda e, g2=gp, h2=hh, v2=vt2: e.tensor_tensor(
                        o_all[:, 0, 2 * h2 + v2, :], o_all[:, 0, 2 * h2 + v2, :],
                        bE[:, 384:512], ALU.mult), ["oall", bEr], ["oall"])
                if yg(23):
                    yield
                S(lambda e, g2=gp, pa=cp, h2=hh: e.activation(dgc[:, g2, :], colstack[:, pa, 32 + h2:33 + h2],
                                                              AF.Exp, scale=-1.0, bias=l127c[:, g2, 0:1]),
                  ["cols%d" % cp, "l127" + gn], ["dg" + gn])
                S(lambda e, g2=gp: e.activation(gamc[:, g2, :], l127c[:, g2, 0:1], AF.Exp),
                  ["l127" + gn], ["gam" + gn])
                V(lambda e, g2=gp: e.tensor_scalar(out=dkb[:, g2, :], in0=ktb[:, g2, :], scalar1=dgc[:, g2, 0:1],
                                                   scalar2=None, op0=ALU.mult), ["kt" + gn, "dg" + gn], ["dk" + gn])
                T(lambda e, g2=gp: e.matmul(bO[:, 256:512], dkb[:, g2, :], ubb[:, g2, :],
                                            start=True, stop=True), ["dk" + gn, "ub" + gn], [bOr])
                if yg(24):
                    yield
                if first_chunk:
                    V(lambda e, sl2=slot, p=bO: e.tensor_copy(stb[:, sl2, :], p[:, 256:512]),
                      [bOr], ["stb"])
                else:
                    V(lambda e, g2=gp, sl2=slot: e.tensor_scalar(
                        out=stb[:, sl2, :], in0=stb[:, sl2, :], scalar1=gamc[:, g2, 0:1], scalar2=None,
                        op0=ALU.mult), ["stb", "gam" + gn], ["stb"])
                    V(lambda e, sl2=slot, p=bO: e.tensor_tensor(
                        stb[:, sl2, :], stb[:, sl2, :], p[:, 256:512], ALU.add),
                      ["stb", bOr], ["stb"])

            gens = [gen_group(hh) for hh in range(H)]
            if cfg.get("nointl"):
                for g in gens:
                    for _ in g:
                        pass
                gens = []
            active = [g for g in gens[0:2]]
            nxt_h = 2
            while active:
                i = 0
                while i < len(active):
                    g = active[i]
                    try:
                        next(g)
                        i += 1
                    except StopIteration:
                        active.pop(i)
                        if nxt_h < H:
                            active.append(gens[nxt_h])
                            nxt_h += 1

            if cfg.get("dump") and b == 0 and blk == 0 and c == 0:
                SD(lambda e: e.dma_start(out=DBG["sgate"].ap(), in_=sgate[:, :, :].rearrange("p a b -> p (a b)")), ["sgate"], ["dbgd"])
                SD(lambda e: e.dma_start(out=DBG["rowstack"].ap(), in_=rowstack[:, 0, :]), ["rows0"], ["dbgd"])
                SD(lambda e: e.dma_start(out=DBG["oall"].ap(), in_=o_all[:, 0, :, :].rearrange("p a b -> p (a b)")), ["oall"], ["dbgd"])
                SD(lambda e: e.dma_start(out=DBG["stb"].ap(), in_=stb[:, :, :].rearrange("p a b -> p (a b)")), ["stb"], ["dbgd"])

            for dh in range(2):
                for vt2 in range(16):
                    T(lambda e, v2=vt2, d2=dh, cc=c: e.matmul(
                        psW[:, 0:512], o_all[:, 0, v2, :], wo_s[:, v2, d2 * 512:(d2 + 1) * 512],
                        start=(v2 == 0), stop=(v2 == 15)), ["oall", "wo_s"], [RB[7]])
                V(lambda e, d2=dh: e.tensor_copy(aw[:, d2 * 512:(d2 + 1) * 512], psW[:, 0:512]),
                  [RB[7]], ["aw"])
            trow = t0 + c * 128
            SD(lambda e, b=b, tr=trow: e.dma_start(out=xres[:, :], in_=X[b, tr:tr + 128, :]),
               ["xio"], ["xio"])
            V(lambda e: e.bn_stats(lnst[:, 0:6], aw[:, 0:512]), ["aw"], ["lnst"])
            V(lambda e: e.bn_stats(lnst[:, 6:12], aw[:, 512:1024]), ["aw"], ["lnst"])
            V(lambda e: e.bn_aggr(mvst[:, 0:2], lnst[:, 0:12]), ["lnst"], ["mvst"], selfsync=True)
            S(lambda e: e.activation(wcols[:, 0:1], mvst[:, 1:2], AF.Ln, bias=cpool[:, 1:2]),
              ["mvst", "cpool"], ["wcols"])
            S(lambda e: e.activation(wcols[:, 0:1], wcols[:, 0:1], AF.Exp, scale=-0.5),
              ["wcols"], ["wcols"])
            V(lambda e: e.tensor_scalar(out=aw[:, :], in0=aw[:, :], scalar1=mvst[:, 0:1],
                                        scalar2=wcols[:, 0:1], op0=ALU.subtract, op1=ALU.mult),
              ["aw", "mvst", "wcols"], ["aw"])
            if cfg.get("dump") and b == 0 and blk == 0 and c == 0:
                SD(lambda e: e.dma_start(out=DBG["aw"].ap(), in_=aw[:, :]), ["aw"], ["dbgd"])
            G(lambda e: e.tensor_tensor(xres[:, :], xres[:, :], aw[:, :], ALU.add), ["xio", "aw"], ["xio"])
            SD(lambda e, b=b, tr=trow: e.dma_start(out=XP[b, tr:tr + 128, :], in_=xres[:, :]),
               ["xio"], ["xp_d"])

    def build_pass2_prep(p2):
        def sb2(name, shape, dt):
            return p2.enter_context(nc.sbuf_tensor(name, shape, dt))
        g1bc = sb2("g1bc", [128, D], F32)
        b1bc = sb2("b1bc", [128, D], F32)
        b2cb = sb2("b2cb", [128, D], F32)
        cgbc = sb2("cgbc", [128, D], F32)
        gb2s = sb2("gb2s", [128, 64], BF16)
        b1col = sb2("b1col", [128, 32], F32)
        g2m = sb2("g2m", [128, 32], F32)
        G1 = sb2("G1", [128, 8, HID], BF16)
        G2 = sb2("G2", [128, 32, 4, 257], BF16)
        xp2 = sb2("xp2", [128, 2, D], F32)
        xn2 = sb2("xn2", [128, D], BF16)
        xn2t = sb2("xn2t", [128, 8, BLK], BF16)
        h_all = sb2("h_all", [128, 32, BLK], BF16)
        h2t = sb2("h2t", [128, 2, 128], BF16)
        fw = sb2("fw", [128, D], F32)
        fw2 = sb2("fw2", [128, D], F32)
        cgcbb = sb2("cgcbb", [1, 2 * D], BF16)
        return dict(g1bc=g1bc, b1bc=b1bc, b2cb=b2cb, cgbc=cgbc, gb2s=gb2s, b1col=b1col,
                    g2m=g2m, G1=G1, G2=G2, xp2=xp2, xn2=xn2, xn2t=xn2t, h_all=h_all,
                    h2t=h2t, fw=fw, fw2=fw2, cgcbb=cgcbb)

    def pass2_prep(B2):
        SD(lambda e: e.dma_start(out=B2["g1bc"][:, :], in_=CG1.ap()), [], ["g1bc"])
        SD(lambda e: e.dma_start(out=B2["b1bc"][:, :], in_=CB1.ap()), [], ["b1bc"])
        SD(lambda e: e.dma_start(out=B2["b2cb"][:, :], in_=CB2.ap()), [], ["b2cb"])
        GD(lambda e: e.dma_start(out=B2["gb2s"][:, :], in_=CGB2.ap()), [], ["gb2s"])
        SD(lambda e: e.dma_start(out=B2["b1col"][:, :], in_=CB1COL.ap()), [], ["b1col"])
        SD(lambda e: e.dma_start(out=B2["g2m"][:, :], in_=CG2COL.ap()), [], ["g2m"])
        for dt_ in range(8):
            SD(lambda e, d=dt_: e.dma_start(out=B2["G1"][:, d, :], in_=TERN1[d * 128:(d + 1) * 128, :]),
               ["tern_d"], ["G1"])
        for ht in range(32):
            SD(lambda e, h2=ht: e.dma_start(
                out=B2["G2"][:, h2, :, 0:256],
                in_=TERN2[h2 * 128:(h2 + 1) * 128, :].rearrange("p (a b) -> p a b", b=256)),
               ["tern_d"], ["G2"])
        V(lambda e: e.memset(B2["G2"][:, :, :, 256:257], 1.0), ["G2"], ["G2"])
        for which, outname in ((0, "cg"), (1, "cb")):
            for pair in range(2):
                for sub in range(2):
                    gr = 2 * pair + sub
                    for ht in range(32):
                        T(lambda e, h2=ht, g2=gr, s2=sub, w2=which: e.matmul(
                            psW[0:1, s2 * 256:(s2 + 1) * 256], B2["gb2s"][:, 2 * h2 + w2:2 * h2 + w2 + 1],
                            B2["G2"][:, h2, g2, 0:256],
                            start=(h2 == 0), stop=(h2 == 31)), ["G2", "gb2s"], [RB[7]])
                V(lambda e, p2=pair, w2=which: e.tensor_copy(
                    B2["cgcbb"][0:1, w2 * D + p2 * 512:w2 * D + (p2 + 1) * 512], psW[0:1, 0:512]),
                  [RB[7]], ["cgcbb"])
        for hf in range(2):
            T(lambda e, hf2=hf: e.matmul(psW[:, 0:512], ones1r[:, :], B2["cgcbb"][0:1, hf2 * 512:(hf2 + 1) * 512],
                                         start=True, stop=True), ["cgcbb", "ones1r"], [RB[7]])
            V(lambda e, hf2=hf: e.tensor_scalar(out=B2["cgbc"][:, hf2 * 512:(hf2 + 1) * 512], in0=psW[:, 0:512],
                                                scalar1=thr[:, 5:6], scalar2=None, op0=ALU.mult),
              [RB[7], "thr"], ["cgbc"])
            T(lambda e, hf2=hf: e.matmul(psW[:, 0:512], ones1r[:, :], B2["cgcbb"][0:1, D + hf2 * 512:D + (hf2 + 1) * 512],
                                         start=True, stop=True), ["cgcbb", "ones1r"], [RB[7]])
            V(lambda e, hf2=hf: e.tensor_scalar(out=B2["fw"][:, hf2 * 512:(hf2 + 1) * 512], in0=psW[:, 0:512],
                                                scalar1=thr[:, 5:6], scalar2=None, op0=ALU.mult),
              [RB[7], "thr"], ["fw"])
        V(lambda e: e.tensor_tensor(B2["b2cb"][:, :], B2["b2cb"][:, :], B2["fw"][:, :], ALU.add),
          ["b2cb", "fw"], ["b2cb"])
        for ht in range(32):
            for gr in range(4):
                V(lambda e, h2=ht, g2=gr: e.tensor_scalar(
                    out=B2["G2"][:, h2, g2, 0:256], in0=B2["G2"][:, h2, g2, 0:256],
                    scalar1=B2["g2m"][:, h2:h2 + 1], scalar2=None, op0=ALU.mult), ["G2", "g2m"], ["G2"],
                  selfsync=True)

    def pass2_block(B2, b, blk):
        t0 = blk * BLK
        for tt in range(2):
            trow = t0 + tt * 128
            SD(lambda e, b2=b, tr=trow, t2=tt: e.dma_start(out=B2["xp2"][:, t2, :], in_=XP[b2, tr:tr + 128, :]),
               ["xp2%d" % tt], ["xp2%d" % tt])
            V(lambda e, t2=tt: e.bn_stats(lnst[:, 0:6], B2["xp2"][:, t2, 0:512]), ["xp2%d" % tt], ["lnst"])
            V(lambda e, t2=tt: e.bn_stats(lnst[:, 6:12], B2["xp2"][:, t2, 512:1024]), ["xp2%d" % tt], ["lnst"])
            V(lambda e: e.bn_aggr(mvst[:, 0:2], lnst[:, 0:12]), ["lnst"], ["mvst"], selfsync=True)
            S(lambda e: e.activation(wcols[:, 0:1], mvst[:, 1:2], AF.Ln, bias=cpool[:, 1:2]),
              ["mvst", "cpool"], ["wcols"])
            S(lambda e: e.activation(wcols[:, 0:1], wcols[:, 0:1], AF.Exp, scale=-0.5), ["wcols"], ["wcols"])
            V(lambda e: e.tensor_scalar(out=wcols[:, 1:2], in0=mvst[:, 1:2], scalar1=wcols[:, 0:1],
                                        scalar2=wcols[:, 0:1], op0=ALU.mult, op1=ALU.mult),
              ["mvst", "wcols"], ["wcols"])
            S(lambda e: e.activation(wcols[:, 1:2], wcols[:, 1:2], AF.Ln, bias=cpool[:, 1:2]),
              ["wcols", "cpool"], ["wcols"])
            S(lambda e: e.activation(wcols[:, 1:2], wcols[:, 1:2], AF.Exp, scale=-0.5), ["wcols"], ["wcols"])
            V(lambda e: e.tensor_scalar(out=wcols[:, 2:3], in0=wcols[:, 0:1], scalar1=wcols[:, 1:2],
                                        scalar2=None, op0=ALU.mult), ["wcols"], ["wcols"])
            V(lambda e, t2=tt: e.tensor_scalar(out=B2["fw"][:, :], in0=B2["xp2"][:, t2, :], scalar1=mvst[:, 0:1],
                                        scalar2=wcols[:, 2:3], op0=ALU.subtract, op1=ALU.mult),
              ["xp2%d" % tt, "mvst", "wcols"], ["fw"])
            V(lambda e: e.tensor_tensor(B2["fw"][:, :], B2["fw"][:, :], B2["g1bc"][:, :], ALU.mult),
              ["fw", "g1bc"], ["fw"])
            G(lambda e: e.tensor_tensor(B2["xn2"][:, :], B2["fw"][:, :], B2["b1bc"][:, :], ALU.add),
              ["fw", "b1bc"], ["xn2"])
            for dt_ in range(8):
                T(lambda e, d=dt_: e.transpose(psT[:, d * 128:(d + 1) * 128],
                                               B2["xn2"][:, d * 128:(d + 1) * 128], ident_b[:, :]),
                  ["xn2", "ident_b"], [RB[4]])
            for dt_ in range(8):
                S(lambda e, d=dt_, t2=tt: e.copy(B2["xn2t"][:, d, t2 * 128:(t2 + 1) * 128],
                                                 psT[:, d * 128:(d + 1) * 128]), [RB[4]], ["xn2t"])
        if cfg.get("dump") and b == 0 and blk == 0:
            SD(lambda e: e.dma_start(out=DBG["thr"].ap(), in_=thr[:, :]), ["thr"], ["dbgd"])
            SD(lambda e: e.dma_start(out=DBG["xn2"].ap(), in_=B2["xn2"][:, :]), ["xn2"], ["dbgd"])
            SD(lambda e: e.dma_start(out=DBG["cgbc"].ap(), in_=B2["cgbc"][:, :]), ["cgbc"], ["dbgd"])
            SD(lambda e: e.dma_start(out=DBG["b2cb"].ap(), in_=B2["b2cb"][:, :]), ["b2cb"], ["dbgd"])
            SD(lambda e: e.dma_start(out=DBG["G1s"].ap(), in_=B2["G1"][:, 0, :]), ["G1"], ["dbgd"])
            SD(lambda e: e.dma_start(out=DBG["G2s"].ap(), in_=B2["G2"][:, 0, :, :].rearrange("p a b -> p (a b)")), ["G2"], ["dbgd"])
        for ht in range(32):
            bank = ht % 2
            ps = psC if bank == 0 else psD
            pr = RB[2 + bank]
            for dt_ in range(8):
                T(lambda e, d=dt_, h2=ht, p=ps: e.matmul(
                    p[:, 0:BLK], B2["G1"][:, d, h2 * 128:(h2 + 1) * 128], B2["xn2t"][:, d, :],
                    start=(d == 0), stop=(d == 7)), ["G1", "xn2t"], [pr])
            S(lambda e, h2=ht, p=ps: e.activation(B2["h_all"][:, h2, :], p[:, 0:BLK], AF.Gelu,
                                                  bias=B2["b1col"][:, h2:h2 + 1]),
              [pr, "b1col"], ["h_all"])
            for tt in range(2):
                S(lambda e, h2=ht, t2=tt: e.activation(B2["h2t"][:, t2, :], B2["h_all"][:, h2, t2 * 128:(t2 + 1) * 128],
                                                       AF.Square), ["h_all"], ["h2t%d" % tt], selfsync=True)
                pss = psU if tt == 0 else psO
                prr = RB[5] if tt == 0 else RB[6]
                T(lambda e, h2=ht, t2=tt, p=pss: e.matmul(
                    p[:, 384:385], B2["h2t"][:, t2, :], onescol[:, :],
                    start=(h2 == 0), stop=(h2 == 31)), ["h2t%d" % tt, "onescol"], [prr])
        pe_fence(psU, RB[5], 386)
        pe_fence(psO, RB[6], 386)
        if cfg.get("dump") and b == 0 and blk == 0:
            V(lambda e: e.tensor_copy(mvst[:, 0:1], psU[:, 384:385]), [RB[5]], ["mvst"])
            V(lambda e: e.tensor_copy(mvst[:, 1:2], psO[:, 384:385]), [RB[6]], ["mvst"])
            SD(lambda e: e.dma_start(out=DBG["sums"].ap(), in_=mvst[:, 0:2]), ["mvst"], ["dbgd"])
        for tt in range(2):
            pss = psU if tt == 0 else psO
            prr = RB[5] if tt == 0 else RB[6]
            for gr in range(4):
                bank = gr % 2
                ps = psA if bank == 0 else psB
                pr = RB[bank]
                for ht in range(32):
                    T(lambda e, h2=ht, g2=gr, t2=tt, p=ps: e.matmul(
                        p[:, 0:257], B2["h_all"][:, h2, t2 * 128:(t2 + 1) * 128], B2["G2"][:, h2, g2, :],
                        start=(h2 == 0), stop=(h2 == 31)), ["h_all", "G2"], [pr])
                if gr == 0:
                    pe_fence(ps, pr, 300)
                    V(lambda e, p=ps: e.tensor_scalar(out=wcols[:, 3:4], in0=p[:, 256:257],
                                                      scalar1=1.0 / HID, scalar2=None, op0=ALU.mult),
                      [pr], ["wcols"])
                    V(lambda e, p=pss: e.tensor_scalar(out=wcols[:, 4:5], in0=p[:, 384:385],
                                                       scalar1=1.0 / HID, scalar2=None, op0=ALU.mult),
                      [prr], ["wcols"])
                    V(lambda e: e.tensor_scalar(out=wcols[:, 5:6], in0=wcols[:, 3:4], scalar1=wcols[:, 3:4],
                                                scalar2=None, op0=ALU.mult), ["wcols"], ["wcols"], selfsync=True)
                    V(lambda e: e.tensor_tensor(wcols[:, 4:5], wcols[:, 4:5], wcols[:, 5:6], ALU.subtract),
                      ["wcols"], ["wcols"])
                    S(lambda e: e.activation(wcols[:, 4:5], wcols[:, 4:5], AF.Ln, bias=cpool[:, 1:2]),
                      ["wcols", "cpool"], ["wcols"])
                    S(lambda e: e.activation(wcols[:, 4:5], wcols[:, 4:5], AF.Exp, scale=-0.5),
                      ["wcols"], ["wcols"])
                    V(lambda e: e.tensor_scalar(out=wcols[:, 6:7], in0=wcols[:, 4:5], scalar1=thr[:, 5:6],
                                                scalar2=None, op0=ALU.mult), ["wcols", "thr"], ["wcols"])
                    V(lambda e: e.tensor_scalar(out=wcols[:, 7:8], in0=wcols[:, 4:5], scalar1=wcols[:, 3:4],
                                                scalar2=None, op0=ALU.mult), ["wcols"], ["wcols"], selfsync=True)
                V(lambda e, g2=gr, p=ps: e.tensor_scalar(
                    out=B2["fw"][:, g2 * 256:(g2 + 1) * 256], in0=p[:, 0:256], scalar1=wcols[:, 6:7],
                    scalar2=None, op0=ALU.mult), [pr, "wcols"], ["fw"], selfsync=True)
            V(lambda e: e.tensor_scalar(out=B2["fw2"][:, :], in0=B2["cgbc"][:, :], scalar1=wcols[:, 7:8],
                                        scalar2=None, op0=ALU.mult), ["cgbc", "wcols"], ["fw2"], selfsync=True)
            G(lambda e: e.tensor_tensor(B2["fw"][:, :], B2["fw"][:, :], B2["fw2"][:, :], ALU.subtract),
              ["fw", "fw2"], ["fw"])
            G(lambda e: e.tensor_tensor(B2["fw"][:, :], B2["fw"][:, :], B2["b2cb"][:, :], ALU.add),
              ["fw", "b2cb"], ["fw"])
            if cfg.get("dump") and b == 0 and blk == 0 and tt == 0:
                SD(lambda e: e.dma_start(out=DBG["fwpre"].ap(), in_=B2["fw"][:, :]), ["fw"], ["dbgd"])
                SD(lambda e: e.dma_start(out=DBG["wcolsd"].ap(), in_=wcols[:, :]), ["wcols"], ["dbgd"])
                SD(lambda e: e.dma_start(out=DBG["hall"].ap(), in_=B2["h_all"][:, :, :].rearrange("p a b -> p (a b)")), ["h_all"], ["dbgd"])
            V(lambda e: e.bn_stats(lnst[:, 0:6], B2["fw"][:, 0:512]), ["fw"], ["lnst"])
            V(lambda e: e.bn_stats(lnst[:, 6:12], B2["fw"][:, 512:1024]), ["fw"], ["lnst"])
            V(lambda e: e.bn_aggr(mvst[:, 0:2], lnst[:, 0:12]), ["lnst"], ["mvst"], selfsync=True)
            S(lambda e: e.activation(wcols[:, 0:1], mvst[:, 1:2], AF.Ln, bias=cpool[:, 1:2]),
              ["mvst", "cpool"], ["wcols"])
            S(lambda e: e.activation(wcols[:, 0:1], wcols[:, 0:1], AF.Exp, scale=-0.5), ["wcols"], ["wcols"])
            V(lambda e: e.tensor_scalar(out=B2["fw"][:, :], in0=B2["fw"][:, :], scalar1=mvst[:, 0:1],
                                        scalar2=wcols[:, 0:1], op0=ALU.subtract, op1=ALU.mult),
              ["fw", "mvst", "wcols"], ["fw"])
            G(lambda e, t2=tt: e.tensor_tensor(B2["fw"][:, :], B2["fw"][:, :], B2["xp2"][:, t2, :], ALU.add),
              ["fw", "xp2%d" % tt], ["fw"])
            trow = t0 + tt * 128
            SD(lambda e, b2=b, tr=trow: e.dma_start(out=OUT[b2, tr:tr + 128, :], in_=B2["fw"][:, :]),
               ["fw"], ["out_d"])

    for b in range(nseq):
        for blk in range(nblk):
            pass1_block(b, blk)
            run_quant_steps(33)
    run_quant_steps(len(quant_steps))
    if do_pass2:
        em.barrier()
        p1.close()
        p2 = contextlib.ExitStack()
        B2 = build_pass2_prep(p2)
        em.paranoid = cfg.get("paranoid2", True)
        pass2_prep(B2)
        for b in range(nseq):
            for blk in range(nblk):
                pass2_block(B2, b, blk)
        em.paranoid = False
        p2.close()
    else:
        p1.close()
    em.barrier()
    em.finalize(nc)
    st.close()
    return nc



def _consts():
    i128 = np.eye(128, dtype=np.float32)
    sl = np.arange(128)
    maskus = np.where(sl[None, :] > sl[:, None], 0.0, -1e30).astype(np.float32)
    maskls = np.where(sl[None, :] < sl[:, None], 0.0, 1e30).astype(np.float32)
    maskui = np.where(sl[None, :] >= sl[:, None], 0.0, -1e30).astype(np.float32)
    ones32 = np.ones((32, 128), np.float32)
    epick = np.zeros((32, 8), np.float32)
    for h in range(8):
        epick[h, h] = 1.0
    selq = np.zeros((128, 64), np.float32)
    for h in range(8):
        selq[:, 8 * h + h] = 1.0
    ones1r = np.ones((1, 128), np.float32)
    onescol = np.ones((128, 1), np.float32)
    cpool = np.zeros((128, 8), np.float32)
    cpool[:, 0] = 1.0
    cpool[:, 1] = 1e-5
    cpool[:, 2] = 1e-12
    cpool[:, 3] = np.log(2.0)
    blkid = np.arange(128) // 16
    mbd = (blkid[:, None] == blkid[None, :]).astype(np.float32)
    msbu = -(blkid[None, :] > blkid[:, None]).astype(np.float32)
    return dict(c_mbd=mbd, c_msbu=msbu, c_ident=i128,
                c_mask3=np.ascontiguousarray(np.concatenate([maskus, maskls, maskui], axis=1)),
                c_ones32=ones32, c_epick=epick, c_selq=selq, c_ones1r=ones1r,
                c_onescol=onescol, c_cpool=cpool)


def make_inmap(inp, b0, nseq, nblk):
    Tloc = nblk * BLK
    cons = _consts()
    convw = np.zeros((128, 128), np.float32)
    cq = inp["L0_attn_conv_q"].reshape(8, 128, 4)
    ck = inp["L0_attn_conv_k"].reshape(8, 128, 4)
    cv = inp["L0_attn_conv_v"].reshape(16, 128, 4)
    for i in range(8):
        convw[:, 4 * i:4 * i + 4] = cq[i]
        convw[:, 32 + 4 * i:32 + 4 * i + 4] = ck[i]
    for i in range(16):
        convw[:, 64 + 4 * i:64 + 4 * i + 4] = cv[i]
    wa = np.zeros((1024, 32), np.float32)
    wa[:, 0:8] = inp["L0_attn_Wa"]
    wb = np.zeros((1024, 32), np.float32)
    wb[:, 0:8] = inp["L0_attn_Wb"]
    abc = np.zeros((32, 2), np.float32)
    abc[0:8, 0] = inp["L0_attn_A_log"]
    abc[0:8, 1] = inp["L0_attn_dt_bias"]
    onorm = np.ascontiguousarray(inp["L0_attn_o_norm_w"].reshape(2, 128).T)
    g1bc = np.ascontiguousarray(np.broadcast_to(inp["L0_ffn_ln1_g"], (128, D)))
    b1bc = np.ascontiguousarray(np.broadcast_to(inp["L0_ffn_ln1_b"], (128, D)))
    b2bc = np.ascontiguousarray(np.broadcast_to(inp["L0_ffn_b2"], (128, D)))
    gb2 = np.ascontiguousarray(np.stack([inp["L0_ffn_ln2_g"], inp["L0_ffn_ln2_b"]], axis=1).reshape(32, 128, 2).transpose(1, 0, 2).reshape(128, 64))
    b1col = np.ascontiguousarray(inp["L0_ffn_b1"].reshape(32, 128).T)
    g2col = np.ascontiguousarray(inp["L0_ffn_ln2_g"].reshape(32, 128).T)
    m = dict(
        x=np.ascontiguousarray(inp["L0_attn_x"][b0:b0 + nseq, :Tloc]),
        wq=inp["L0_attn_Wq"], wk=inp["L0_attn_Wk"], wv=inp["L0_attn_Wv"],
        wg=inp["L0_attn_Wg"], wa=wa, wb=wb, wo=inp["L0_attn_Wo"],
        w1t=np.ascontiguousarray(inp["L0_ffn_w1"].T),
        w2t=np.ascontiguousarray(inp["L0_ffn_w2"].T),
        c_convw=convw, c_abc=abc, c_onorm=onorm,
        c_g1bc=g1bc, c_b1bc=b1bc, c_b2bc=b2bc, c_gb2=gb2,
        c_b1col=b1col, c_g2col=g2col,
    )
    m.update(cons)
    return m


def kernel(**inp):
    cfg = dict(nseq=2, nblk=8, do_pass2=True)
    nc = build_nc(cfg)
    in_maps = [make_inmap(inp, 2 * c, cfg["nseq"], cfg["nblk"]) for c in range(NCORES)]
    res = run_bass_kernel_spmd(nc, in_maps, core_ids=list(range(NCORES)))
    out = np.concatenate([r["out"] for r in res.results], axis=0)
    return out.astype(np.float32)
